# version 06
# baseline (speedup 1.0000x reference)
import contextlib
import math

import numpy as np
import ml_dtypes

import concourse.bass as bass
import concourse.mybir as mybir
from concourse.alu_op_type import AluOpType as OP
from concourse.bass_utils import run_bass_kernel_spmd

F32 = mybir.dt.float32
BF16 = mybir.dt.bfloat16
U32 = mybir.dt.uint32
AF = mybir.ActivationFunctionType
AXX = mybir.AxisListType.X
BF = ml_dtypes.bfloat16

B = 2
S = 4096
DM = 4096
NH, NKV, HD = 32, 8, 128
T = B * S
TPC = T // 8
INTER = 11008
FRW = INTER // 8
FRP = 1408
IPAD = 8 * FRP
NFPC = IPAD // 128
NG = 32
NGD = 86
INV_SQRT_HD = 1.0 / math.sqrt(HD)
EPS = 1e-5
NEG = -1.0e9

import os
STOP_PHASE = int(os.environ.get("KSTOP", "99"))
KQKV = int(os.environ.get("KQKV", "9"))



def _host_prep(inputs):
    x = np.asarray(inputs["L0_attn_hidden_states"], dtype=np.float32)
    Wq = np.asarray(inputs["L0_attn_Wq"], dtype=np.float32)
    Wk = np.asarray(inputs["L0_attn_Wk"], dtype=np.float32)
    Wv = np.asarray(inputs["L0_attn_Wv"], dtype=np.float32)
    Wo = np.asarray(inputs["L0_attn_Wo"], dtype=np.float32)
    qwg = np.asarray(inputs["L0_ffn_qweight_gu"])
    qzg = np.asarray(inputs["L0_ffn_qzeros_gu"])
    sg = np.asarray(inputs["L0_ffn_scales_gu"], dtype=np.float32)
    qwd = np.asarray(inputs["L0_ffn_qweight_down"])
    qzd = np.asarray(inputs["L0_ffn_qzeros_down"])
    sd = np.asarray(inputs["L0_ffn_scales_down"], dtype=np.float32)
    g_gu = np.asarray(inputs["L0_ffn_g_idx_gu"]).astype(np.int64)
    g_dn = np.asarray(inputs["L0_ffn_g_idx_down"]).astype(np.int64)

    xf = x.reshape(T, DM)
    xT = np.ascontiguousarray(xf.T)

    n_idx = np.arange(IPAD) // FRP
    fr_idx = np.arange(IPAD) % FRP
    valid = fr_idx < FRW
    f_of_slot = np.where(valid, 8 * fr_idx + n_idx, 0)

    qwg_gate = np.ascontiguousarray(qwg[:, f_of_slot])
    qwg_up = np.ascontiguousarray(qwg[:, INTER + f_of_slot])
    s_slot = np.zeros((NG, 2 * IPAD), dtype=np.float32)
    s_slot[:, :IPAD] = np.where(valid[None, :], sg[:, f_of_slot], 0.0)
    s_slot[:, IPAD:] = np.where(valid[None, :], sg[:, INTER + f_of_slot], 0.0)

    pos = np.arange(S, dtype=np.float32)
    freqs = np.exp(-np.arange(0, HD, 2, dtype=np.float32)
                   * (math.log(10000.0) / HD))
    ang = pos[:, None] * freqs[None, :]
    cos_t = np.zeros((HD, S), dtype=np.float32)
    sin_t = np.zeros((HD, S), dtype=np.float32)
    cos_t[:64] = np.cos(ang).T
    cos_t[64:] = np.cos(ang).T
    sin_t[:64] = np.sin(ang).T
    sin_t[64:] = np.sin(ang).T

    ident = np.eye(128, dtype=BF)
    permrot = np.zeros((128, 128), dtype=np.float32)
    for m in range(64):
        permrot[m + 64, m] = -1.0
    for m in range(64, 128):
        permrot[m - 64, m] = 1.0
    permrot = permrot.astype(BF)

    masks = np.zeros((128, 4 * 512), dtype=np.float32)
    qq = np.arange(128)[:, None]
    kk = np.arange(512)[None, :]
    for r in range(4):
        masks[:, r * 512:(r + 1) * 512] = np.where(kk <= 128 * r + qq, 0.0, NEG)

    gones = np.zeros((128, 128), dtype=np.float32)
    for wc in range(4):
        for p in range(128):
            g = g_gu[8 * (128 * wc + p)]
            gones[p, 32 * wc + g] = 1.0
    gones = gones.astype(BF)

    dones = np.zeros((128, 11 * 88), dtype=np.float32)
    for c in range(11):
        for p in range(128):
            fr = 128 * c + p
            if fr < FRW:
                dones[p, c * 88 + g_dn[8 * fr]] = 1.0
    dones = dones.astype(BF)

    shared = {
        "xT": xT, "cos_t": cos_t, "sin_t": sin_t,
        "ident": ident, "permrot": permrot, "masks": masks,
        "gones": gones, "dones": dones, "s_slot": s_slot,
        "qwg_gate": qwg_gate, "qwg_up": qwg_up,
        "qzg": qzg, "qwd": np.ascontiguousarray(qwd), "qzd": qzd,
        "sdown": np.ascontiguousarray(sd),
    }

    WqT = np.ascontiguousarray(Wq.T)
    WkT = np.ascontiguousarray(Wk.T)
    WvT = np.ascontiguousarray(Wv.T)
    WoT = np.ascontiguousarray(Wo.T)

    per_core = []
    for c in range(8):
        wkvT = np.concatenate(
            [WqT[:, 512 * c:512 * (c + 1)],
             WkT[:, 128 * c:128 * (c + 1)],
             WvT[:, 128 * c:128 * (c + 1)]], axis=1)
        d = dict(shared)
        d["x_tok"] = np.ascontiguousarray(xf[TPC * c:TPC * (c + 1)])
        d["wkvT"] = np.ascontiguousarray(wkvT)
        d["woT"] = np.ascontiguousarray(WoT[512 * c:512 * (c + 1), :])
        per_core.append(d)
    return per_core


class Ctr:
    def __init__(self, h):
        self.h = h
        self.v = 0


def build_program():
    nc = bass.Bass("TRN2", target_bir_lowering=False)

    ap = {}
    def din(name, shape, dt):
        ap[name] = nc.dram_tensor(name, shape, dt, kind="ExternalInput").ap()
    din("xT", [DM, T], F32)
    din("x_tok", [TPC, DM], F32)
    din("wkvT", [DM, 768], F32)
    din("woT", [512, DM], F32)
    din("cos_t", [HD, S], F32)
    din("sin_t", [HD, S], F32)
    din("ident", [128, 128], BF16)
    din("permrot", [128, 128], BF16)
    din("masks", [128, 2048], F32)
    din("gones", [128, 128], BF16)
    din("dones", [128, 11 * 88], BF16)
    din("s_slot", [NG, 2 * IPAD], F32)
    din("qwg_gate", [512, IPAD], U32)
    din("qwg_up", [512, IPAD], U32)
    din("qzg", [NG, 2752], U32)
    din("qwd", [FRW, DM], U32)
    din("qzd", [NGD, 512], U32)
    din("sdown", [NGD, DM], F32)

    y = nc.dram_tensor("y", [TPC, DM], F32, kind="ExternalOutput").ap()
    KDBG = int(os.environ.get("KDBG", "0"))
    dbg = {"mask": KDBG}
    if KDBG:
        dbg["d_ccout"] = nc.dram_tensor("d_ccout", [DM, TPC], BF16,
                                        kind="ExternalOutput").ap()
        dbg["d_ccin"] = nc.dram_tensor("d_ccin", [DM, TPC], BF16,
                                       kind="ExternalOutput").ap()
        dbg["d_x1"] = nc.dram_tensor("d_x1", [TPC, DM], F32,
                                     kind="ExternalOutput").ap()
        dbg["d_h"] = nc.dram_tensor("d_h", [2, 128, TPC], BF16,
                                    kind="ExternalOutput").ap()
        dbg["d_x1tp"] = nc.dram_tensor("d_x1tp", [128, TPC], BF16,
                                       kind="ExternalOutput").ap()
        dbg["d_wg"] = nc.dram_tensor("d_wg", [128, 512], BF16,
                                     kind="ExternalOutput").ap()
        dbg["d_ag"] = nc.dram_tensor("d_ag", [128, 512], F32,
                                     kind="ExternalOutput").ap()
        dbg["d_xg"] = nc.dram_tensor("d_xg", [32, TPC], BF16,
                                     kind="ExternalOutput").ap()
        dbg["d_bg"] = nc.dram_tensor("d_bg", [32, 2 * FRP], BF16,
                                     kind="ExternalOutput").ap()
        dbg["d_wdd"] = nc.dram_tensor("d_wdd", [2, 128, DM], BF16,
                                      kind="ExternalOutput").ap()
        dbg["d_yf"] = nc.dram_tensor("d_yf", [2, 128, 512], BF16,
                                     kind="ExternalOutput").ap()
        dbg["d_bd"] = nc.dram_tensor("d_bd", [88, 512], BF16,
                                     kind="ExternalOutput").ap()
        dbg["d_zd"] = nc.dram_tensor("d_zd", [88, 512], U32,
                                     kind="ExternalOutput").ap()
        dbg["d_hg"] = nc.dram_tensor("d_hg", [88, 512], BF16,
                                     kind="ExternalOutput").ap()
        dbg["d_hfull"] = nc.dram_tensor("d_hfull", [NFPC, 128, TPC], BF16,
                                        kind="ExternalOutput").ap()

    cc_in = nc.dram_tensor("cc_in", [8 * DM, TPC], BF16).ap()
    cc_out = nc.dram_tensor("cc_out", [DM, TPC], BF16).ap()
    x1d = nc.dram_tensor("x1d", [TPC, DM], F32).ap()
    hdram = nc.dram_tensor("hdram", [NFPC, 128, TPC], BF16).ap()
    wdd = nc.dram_tensor("wdd", [NFPC, 128, DM], BF16).ap()

    with nc.Block() as blk:
        stack = contextlib.ExitStack()
        sems = {}
        names = ["s_const", "s_wkv", "s_x", "s_rope", "s_qkps", "s_ev",
                 "s_rp", "s_pe2", "s_sc", "s_ve2", "s_wo", "s_st", "s_cc",
                 "s_dd", "s_ddv", "s_wdd", "s_p4", "s_p4l", "s_p5l", "s_p5v",
                 "s_p5p", "s_p5s", "s_hst", "s_p6l", "s_p6p", "s_p6s",
                 "s_p6v", "s_out", "s_bar", "s_p4s", "s_p4t", "s_p6sq",
                 "s_woe", "s_msk", "s_st2", "s_wdd2", "s_dd2", "s_ddb",
                 "s_p6h", "s_p6sd", "s_p6x", "s_p6w0", "s_p6w1", "s_p6w2",
                 "s_x2", "s_x3", "s_x4"]
        for nm in names:
            sems[nm] = Ctr(stack.enter_context(nc.semaphore(nm)))
        with stack:
            _emit(nc, blk, ap, y, cc_in, cc_out, x1d, hdram, wdd, sems, dbg)
    return nc


def _barrier(blk, s_bar):
    @blk.tensor
    def _(t):
        t.drain().then_inc(s_bar.h, 1)
    @blk.vector
    def _(v):
        v.drain().then_inc(s_bar.h, 1)
    @blk.scalar
    def _(sc):
        sc.drain().then_inc(s_bar.h, 1)
    s_bar.v += 3
    tgt = s_bar.v
    @blk.tensor
    def _(t):
        t.wait_ge(s_bar.h, tgt)
    @blk.vector
    def _(v):
        v.wait_ge(s_bar.h, tgt)
    @blk.scalar
    def _(sc):
        sc.wait_ge(s_bar.h, tgt)
    @blk.sync
    def _(sy):
        sy.wait_ge(s_bar.h, tgt)
    @blk.gpsimd
    def _(g):
        g.wait_ge(s_bar.h, tgt)


def _emit(nc, blk, ap, y, cc_in, cc_out, x1d, hdram, wdd, sm, dbg=None):
    s_const = sm["s_const"]; s_wkv = sm["s_wkv"]; s_x = sm["s_x"]
    s_rope = sm["s_rope"]; s_qkps = sm["s_qkps"]; s_ev = sm["s_ev"]
    s_rp = sm["s_rp"]; s_pe2 = sm["s_pe2"]; s_sc = sm["s_sc"]
    s_ve2 = sm["s_ve2"]; s_wo = sm["s_wo"]; s_st = sm["s_st"]
    s_cc = sm["s_cc"]; s_dd = sm["s_dd"]; s_ddv = sm["s_ddv"]
    s_wdd = sm["s_wdd"]; s_p4 = sm["s_p4"]; s_p4l = sm["s_p4l"]
    s_p5l = sm["s_p5l"]; s_p5v = sm["s_p5v"]; s_p5p = sm["s_p5p"]
    s_p5s = sm["s_p5s"]; s_hst = sm["s_hst"]; s_p6l = sm["s_p6l"]
    s_p6p = sm["s_p6p"]; s_p6s = sm["s_p6s"]; s_p6v = sm["s_p6v"]
    s_out = sm["s_out"]; s_bar = sm["s_bar"]
    s_p4s = sm["s_p4s"]; s_p4t = sm["s_p4t"]; s_p6sq = sm["s_p6sq"]

    st2 = contextlib.ExitStack()
    with st2:
        ec = st2.enter_context
        ident = ec(nc.sbuf_tensor("identb", [128, 128], BF16))
        prot = ec(nc.sbuf_tensor("protb", [128, 128], BF16))
        gonesb = ec(nc.sbuf_tensor("gonesb", [128, 128], BF16))
        donesb = ec(nc.sbuf_tensor("donesb", [128, 11 * 88], BF16))
        stats = ec(nc.sbuf_tensor("stats", [128, 64], F32))
        mv = ec(nc.sbuf_tensor("mvt", [128, 2], F32))
        veps = ec(nc.sbuf_tensor("veps", [128, 1], F32))
        stdv = ec(nc.sbuf_tensor("stdv", [128, 1], F32))
        rstd = ec(nc.sbuf_tensor("rstd", [128, 1], F32))
        rmax = ec(nc.sbuf_tensor("rmax", [128, 4], F32))
        nmax = ec(nc.sbuf_tensor("nmax", [128, 4], F32))
        ssum = ec(nc.sbuf_tensor("ssum", [128, 4], F32))
        rsum = ec(nc.sbuf_tensor("rsum", [128, 4], F32))

        @blk.sync
        def _(sy):
            for nm, t in [("ident", ident), ("permrot", prot),
                          ("gones", gonesb), ("dones", donesb)]:
                sy.dma_start(out=t[:, :], in_=ap[nm][:, :]).then_inc(s_const.h, 16)
                s_const.v += 16
        CONST = s_const.v

        st3 = contextlib.ExitStack()
        with st3:
            ec3 = st3.enter_context
            arena = ec3(nc.sbuf_tensor("arena", [128, 40960], BF16))
            scrx = ec3(nc.sbuf_tensor("scrx", [128, 4096], F32))
            maskb = ec3(nc.sbuf_tensor("maskb", [128, 2048], F32))
            q_sb = ec3(nc.sbuf_tensor("q_sb", [128, 4 * S], BF16))
            k_sb = ec3(nc.sbuf_tensor("k_sb", [128, S], BF16))
            vtok = ec3(nc.sbuf_tensor("vtok", [128, S], BF16))
            prT = ec3(nc.sbuf_tensor("prT", [128, 2 * 512], BF16))
            ropes = ec3(nc.sbuf_tensor("ropes", [128, 1024], F32))
            qraw = ec3(nc.sbuf_tensor("qraw", [128, 6 * 512], BF16))
            rt1 = ec3(nc.sbuf_tensor("rt1", [128, 512], F32))
            rt2 = ec3(nc.sbuf_tensor("rt2", [128, 512], F32))
            wosb = ec3(nc.sbuf_tensor("wosb", [128, 2 * 512], BF16))

            ps = [ec3(nc.psum_tensor(f"ps{i}", [128, 512], F32))
                  for i in range(6)]
            psb = ec3(nc.psum_tensor("psb", [128, 512], BF16))
            psb2 = ec3(nc.psum_tensor("psb2", [128, 512], BF16))
            psbp = [psb, psb2]

            @blk.sync
            def _(sy):
                sy.dma_start(out=maskb[:, :], in_=ap["masks"][:, :]
                             ).then_inc(sm["s_msk"].h, 16)
                sm["s_msk"].v += 16
            sm["s_msk"].mask_mark = sm["s_msk"].v

            wkv = arena[:, 0:24576]
            xst = arena[:, 24576:40960]
            scr0 = scrx[:, :]
            scr1 = arena[:, 0:8192].bitcast(F32)
            prb = arena[:, 8192:24576]
            outT = arena[:, 24576:40960]
            woB = arena[:, 0:16384]

            nbatch = 0 if STOP_PHASE < 1 else (1 if STOP_PHASE < 4 else B)
            for b in range(nbatch):
                _qkv(nc, blk, ap, b, wkv, xst, ropes, q_sb, k_sb, vtok,
                     qraw, rt1, rt2, ps, psb, ident, prot, sm, CONST)
                if STOP_PHASE >= 2:
                    _attn(nc, blk, b, q_sb, k_sb, vtok, outT, [scr0, scr1],
                          prb, prT, maskb, rmax, nmax, ssum, rsum, ps, psbp,
                          ident, sm)
                if STOP_PHASE >= 3:
                    _wo(nc, blk, ap, b, outT, woB, wosb, cc_in, ps, sm)

        _barrier(blk, s_bar)

        if STOP_PHASE < 5:
            _dummy_out(nc, blk, y, s_out)
            return

        @blk.gpsimd
        def _(g):
            g.wait_ge(s_st.h, s_st.v)
            g.wait_ge(sm["s_st2"].h, sm["s_st2"].v)
            g.collective_compute(
                "ReduceScatter", OP.add,
                replica_groups=[list(range(8))],
                ins=[cc_in.opt()],
                outs=[cc_out.opt()],
            ).then_inc(s_cc.h, 1)
            s_cc.v += 1

        if dbg and (dbg["mask"] & 1):
            @blk.sync
            def _(sy):
                sy.wait_ge(sm["s_st"].h, sm["s_st"].v)
                sy.dma_start(out=dbg["d_ccin"][:, :], in_=cc_in[0:DM, :]
                             ).then_inc(sm["s_out"].h, 16)
                sm["s_out"].v += 16
                sy.wait_ge(sm["s_cc"].h, 1)
                sy.dma_start(out=dbg["d_ccout"][:, :], in_=cc_out[:, :]
                             ).then_inc(sm["s_out"].h, 16)
                sm["s_out"].v += 16

        st4 = contextlib.ExitStack()
        with st4:
            ec4 = st4.enter_context
            qwdb = [ec4(nc.sbuf_tensor(f"qwdb{i}", [128, DM], U32))
                    for i in range(2)]
            add_t = ec4(nc.sbuf_tensor("add_t", [128, DM], F32))
            nibd = ec4(nc.sbuf_tensor("nibd", [128, DM], U32))
            wdt = ec4(nc.sbuf_tensor("wdt", [128, 2 * DM], BF16))
            wdt_store = [0, 0]

            @blk.vector
            def _(v):
                v.memset(add_t[96:128, :], 0.0)
                v.drain().then_inc(s_ddv.h, 1)
                s_ddv.v += 1

            ST_ALL = s_st.v
            s_dd2 = sm["s_dd2"]; s_ddb = sm["s_ddb"]; s_wdd2 = sm["s_wdd2"]
            dd_sems = [s_dd, s_dd2]
            wdd_sems = [s_wdd, s_wdd2]
            for c in range(11):
                rows = 128 if c < 10 else FRW - 1280
                buf = qwdb[c % 2]
                bsem = dd_sems[c % 2]
                @blk.sync
                def _(sy, c=c, rows=rows, buf=buf, bsem=bsem):
                    if c == 0:
                        sy.wait_ge(s_st.h, ST_ALL)
                        sy.wait_ge(sm["s_st2"].h, sm["s_st2"].v)
                    if c >= 2:
                        sy.wait_ge(s_ddv.h, 1 + 8 * (c - 1))
                    sy.dma_start(out=buf[0:rows, :],
                                 in_=ap["qwd"][128 * c:128 * c + rows, :]
                                 ).then_inc(bsem.h, 16)
                    bsem.v += 16

                @blk.gpsimd
                def _(g, c=c):
                    g.wait_ge(s_ddv.h, 1 + (8 * c if c > 0 else 0))
                    glo = 8 * c
                    gn = min(NGD - glo, 8)
                    g.dma_start(
                        out=add_t[0:16 * gn, :],
                        in_=ap["sdown"][glo:glo + gn, :].unsqueeze(1)
                        .broadcast_to([gn, 16, DM])).then_inc(s_ddb.h, 16)
                    s_ddb.v += 16
                LD = bsem.v
                LB = s_ddb.v

                for n in range(8):
                    sl = (8 * c + n) % 2
                    @blk.vector
                    def _(v, c=c, n=n, rows=rows, buf=buf, LD=LD, LB=LB,
                          sl=sl, bsem=bsem, need_st=wdt_store[sl]):
                        v.wait_ge(bsem.h, LD)
                        v.wait_ge(s_ddb.h, LB)
                        if need_st > 0:
                            v.wait_ge(wdd_sems[sl].h, need_st)
                        v.tensor_scalar(nibd[:, :], buf[:, :],
                                        28 - 4 * n, 28,
                                        OP.logical_shift_left,
                                        OP.logical_shift_right)
                        v.drain()
                        v.tensor_tensor(wdt[:, DM * sl:DM * sl + DM],
                                        nibd[:, :],
                                        add_t[:, :], OP.mult)
                        v.drain().then_inc(s_ddv.h, 1)
                        s_ddv.v += 1

                    @blk.sync
                    def _(sy, c=c, n=n, sl=sl):
                        sy.wait_ge(s_ddv.h, s_ddv.v)
                        sy.dma_start(out=wdd[n * 11 + c, :, :],
                                     in_=wdt[:, DM * sl:DM * sl + DM]
                                     ).then_inc(wdd_sems[sl].h, 16)
                        wdd_sems[sl].v += 16
                    wdt_store[sl] = wdd_sems[sl].v

        if STOP_PHASE < 6:
            _barrier(blk, s_bar)
            _dummy_out(nc, blk, y, s_out, wait=[(sm["s_cc"].h, 1),
                                               (sm["s_wdd"].h, sm["s_wdd"].v)])
            return

        stP45 = contextlib.ExitStack()
        stP45.__enter__()
        x1tp = stP45.enter_context(nc.sbuf_tensor("x1tp", [128, 32 * TPC], BF16))
        xgT = stP45.enter_context(nc.sbuf_tensor("xgT", [32, TPC], BF16))
        xgL = stP45.enter_context(nc.sbuf_tensor("xgL", [32, TPC], BF16))
        xgF = stP45.enter_context(nc.sbuf_tensor("xgF", [32, TPC], F32))
        st5 = contextlib.ExitStack()
        with st5:
            ec5 = st5.enter_context
            at_tok = ec5(nc.sbuf_tensor("at_tok", [128, DM], BF16))
            xr = ec5(nc.sbuf_tensor("xr", [128, DM], F32))
            x1f = ec5(nc.sbuf_tensor("x1f", [128, DM], F32))
            x1b = ec5(nc.sbuf_tensor("x1b", [128, DM], BF16))
            x1bp = ec5(nc.sbuf_tensor("x1bp", [128, DM], BF16))
            psg = ec5(nc.psum_tensor("psg", [32, 512], F32))
            psb4 = [ec5(nc.psum_tensor(f"psb4_{i}", [128, 512], BF16))
                    for i in range(2)]

            for tt in range(8):
                @blk.sync
                def _(sy, tt=tt):
                    if tt == 0:
                        sy.wait_ge(s_cc.h, 1)
                        sy.wait_ge(s_wdd.h, s_wdd.v)
                        sy.wait_ge(sm["s_wdd2"].h, sm["s_wdd2"].v)
                        sy.wait_ge(s_dd.h, s_dd.v)
                        sy.wait_ge(sm["s_dd2"].h, sm["s_dd2"].v)
                        sy.wait_ge(sm["s_ddb"].h, sm["s_ddb"].v)
                    else:
                        sy.wait_ge(s_p4.h, s_p4.v)
                    sy.dma_start(out=at_tok[:, :],
                                 in_=cc_out[:, 128 * tt:128 * (tt + 1)],
                                 transpose=True).then_inc(s_p4l.h, 16)
                    s_p4l.v += 16
                    sy.dma_start(out=xr[:, :],
                                 in_=ap["x_tok"][128 * tt:128 * (tt + 1), :]
                                 ).then_inc(s_p4l.h, 16)
                    s_p4l.v += 16

                @blk.vector
                def _(v, tt=tt):
                    v.wait_ge(s_p4l.h, s_p4l.v)
                    v.tensor_tensor(xr[:, :], xr[:, :], at_tok[:, :], OP.add)
                    v.drain()
                    for j in range(8):
                        v.bn_stats(stats[:, 6 * j:6 * (j + 1)],
                                   xr[:, 512 * j:512 * (j + 1)])
                    v.drain()
                    v.bn_aggr(mv[:, :], stats[:, 0:48])
                    v.drain()
                    v.tensor_scalar(veps[:, :], mv[:, 1:2], EPS, None, OP.add)
                    v.drain().then_inc(s_p4.h, 1)
                    s_p4.v += 1

                @blk.scalar
                def _(sc):
                    sc.wait_ge(s_p4.h, s_p4.v)
                    sc.activation(stdv[:, :], veps[:, :], AF.Sqrt).then_inc(s_p4s.h, 1)
                    s_p4s.v += 1

                @blk.vector
                def _(v, tt=tt, SQ=s_p4s.v, STM=s_st.v):
                    v.wait_ge(s_p4s.h, SQ)
                    if tt > 0:
                        v.wait_ge(s_st.h, STM)
                    v.reciprocal(rstd[:, :], stdv[:, :])
                    v.drain()
                    v.tensor_scalar(x1f[:, :], xr[:, :], mv[:, 0:1], rstd[:, :],
                                    OP.subtract, OP.mult)
                    v.drain()
                    v.tensor_copy(x1b[:, :], x1f[:, :])
                    v.drain()
                    v.tensor_copy(
                        x1bp[:, :].rearrange("p (nn wr) -> p nn wr", nn=8),
                        x1b[:, :].rearrange("p (wr nn) -> p nn wr", nn=8))
                    v.drain().then_inc(s_p4.h, 1)
                    s_p4.v += 1

                @blk.sync
                def _(sy, tt=tt):
                    sy.wait_ge(s_p4.h, s_p4.v)
                    sy.dma_start(out=x1d[128 * tt:128 * (tt + 1), :],
                                 in_=x1f[:, :]).then_inc(s_st.h, 16)
                    s_st.v += 16

                X1B = s_p4.v
                if int(os.environ.get("KP4", "9")) < 2:
                    continue
                for hc in range(32):
                    n, wc = hc // 4, hc % 4
                    @blk.tensor
                    def _(t, tt=tt, hc=hc, n=n, wc=wc, X1B=X1B, CP=s_p4.v):
                        if tt == 0 and hc == 0:
                            t.wait_ge(s_const.h, CONST)
                        t.wait_ge(s_p4.h, X1B)
                        if hc >= 2 or tt > 0:
                            t.wait_ge(s_p4.h, CP - 1)
                        t.transpose(psb4[hc % 2][:, 0:128],
                                    x1bp[:, 128 * hc:128 * (hc + 1)],
                                    ident[:, :]).then_inc(s_p4t.h, 1)
                        s_p4t.v += 1

                    if int(os.environ.get("KP4", "9")) < 3:
                        @blk.vector
                        def _(v):
                            v.engine_nop().then_inc(s_p4.h, 1)
                            s_p4.v += 1
                        continue
                    @blk.vector
                    def _(v, tt=tt, hc=hc, TR=s_p4t.v):
                        v.wait_ge(s_p4t.h, TR)
                        v.tensor_copy(
                            x1tp[:, TPC * hc + 128 * tt:TPC * hc + 128 * (tt + 1)],
                            psb4[hc % 2][:, 0:128]).then_inc(s_p4.h, 1)
                        s_p4.v += 1

            if int(os.environ.get("KP4", "9")) < 4:
                _barrier(blk, s_bar)
                _dummy_out(nc, blk, y, s_out)
                stP45.__exit__(None, None, None)
                return
            for th in range(2):
                @blk.tensor
                def _(t, th=th, CP=s_p4.v):
                    t.wait_ge(s_p4.h, CP)
                    if th == 1:
                        t.wait_ge(s_p4.h, s_p4.v)
                    for hc in range(32):
                        wc = hc % 4
                        mmi = t.matmul(psg[:, :],
                                 gonesb[:, 32 * wc:32 * (wc + 1)],
                                 x1tp[:, TPC * hc + 512 * th:TPC * hc + 512 * (th + 1)],
                                 start=(hc == 0), stop=(hc == 31))
                    mmi.then_inc(s_p4t.h, 1)
                    s_p4t.v += 1

                @blk.vector
                def _(v, th=th, MG=s_p4t.v):
                    v.wait_ge(s_p4t.h, MG)
                    sl5 = slice(512 * th, 512 * (th + 1))
                    v.tensor_scalar(xgF[:, sl5], psg[:, :],
                                    -1.0, None, OP.mult)
                    v.drain()
                    v.tensor_copy(xgT[:, sl5], xgF[:, sl5])
                    v.drain()
                    v.tensor_tensor(xgF[:, sl5], xgF[:, sl5], xgT[:, sl5],
                                    OP.subtract)
                    v.drain()
                    v.tensor_copy(xgL[:, sl5], xgF[:, sl5])
                    v.drain().then_inc(s_p4.h, 1)
                    s_p4.v += 1

        _barrier(blk, s_bar)

        if STOP_PHASE < 7:
            _dummy_out(nc, blk, y, s_out)
            stP45.__exit__(None, None, None)
            return

        st6 = contextlib.ExitStack()
        with st6:
            ec6 = st6.enter_context
            qg_sb = ec6(nc.sbuf_tensor("qg_sb", [128, 4 * 512], U32))
            qu_sb = ec6(nc.sbuf_tensor("qu_sb", [128, 4 * 512], U32))
            ag_sb = ec6(nc.sbuf_tensor("ag_sb", [128, 4 * 512], F32))
            au_sb = ec6(nc.sbuf_tensor("au_sb", [128, 4 * 512], F32))
            wg_sb = ec6(nc.sbuf_tensor("wg_sb", [128, 32 * 512], BF16))
            wu_sb = ec6(nc.sbuf_tensor("wu_sb", [128, 32 * 512], BF16))
            nib5 = ec6(nc.sbuf_tensor("nib5", [128, 1024], U32))
            sgate = ec6(nc.sbuf_tensor("sgate", [128, 512], BF16))
            h_sb = ec6(nc.sbuf_tensor("h_sb", [128, 2 * 512], BF16))
            zrow = ec6(nc.sbuf_tensor("zrow", [32, 512], U32))
            srow = ec6(nc.sbuf_tensor("srow", [32, 1024], F32))
            bg_sb2 = ec6(nc.sbuf_tensor("bg_sb2", [32, 4 * FRP], BF16))
            qz_sb = ec6(nc.sbuf_tensor("qz_sb", [32, 1024], U32))

            ps5 = [ec6(nc.psum_tensor(f"ps5_{i}", [128, 512], F32))
                   for i in range(4)]

            @blk.sync
            def _(sy):
                sy.wait_ge(s_wdd.h, s_wdd.v)
                sy.wait_ge(sm["s_wdd2"].h, sm["s_wdd2"].v)
                sy.wait_ge(s_dd.h, s_dd.v)
                sy.wait_ge(sm["s_dd2"].h, sm["s_dd2"].v)
                sy.wait_ge(s_st.h, s_st.v)

            @blk.gpsimd
            def _(g):
                g.wait_ge(sm["s_ddb"].h, sm["s_ddb"].v)
                g.wait_ge(s_st.h, s_st.v)
                g.wait_ge(s_p4l.h, s_p4l.v)

            franges = [(0, 512), (512, 512), (1024, 384)]
            first_group = True
            pair_done = [None, None]
            sg_used = None
            gidx = 0
            DQB_prev = 0
            for (fr0, frw) in franges:
                vw = min(FRW - fr0, frw)
                for n in range(8):
                    slot0 = n * FRP + fr0
                    bg_sb = bg_sb2[:, (2 * FRP) * (gidx % 2):(2 * FRP) * (gidx % 2) + 2 * FRP]
                    gidx += 1
                    MM_PREV = s_p5p.v
                    @blk.sync
                    def _(sy, fr0=fr0, frw=frw, n=n, slot0=slot0,
                          first=first_group, DQB_prev=DQB_prev,
                          vw_0=vw):
                        if not first:
                            sy.wait_ge(s_p5v.h, DQB_prev)
                        for wc in range(4):
                            sy.dma_start(
                                out=qg_sb[:, 512 * wc:512 * wc + frw],
                                in_=ap["qwg_gate"][128 * wc:128 * (wc + 1),
                                                   slot0:slot0 + frw]
                            ).then_inc(s_p5l.h, 16)
                            s_p5l.v += 16
                            sy.dma_start(
                                out=qu_sb[:, 512 * wc:512 * wc + frw],
                                in_=ap["qwg_up"][128 * wc:128 * (wc + 1),
                                                 slot0:slot0 + frw]
                            ).then_inc(s_p5l.h, 16)
                            s_p5l.v += 16
                        sy.dma_start(out=srow[:, 0:vw_0],
                                     in_=ap["s_slot"][:, slot0:slot0 + vw_0]
                                     ).then_inc(s_p5l.h, 16)
                        s_p5l.v += 16
                        sy.dma_start(out=srow[:, 512:512 + vw_0],
                                     in_=ap["s_slot"][:, IPAD + slot0:IPAD + slot0 + vw_0]
                                     ).then_inc(s_p5l.h, 16)
                        s_p5l.v += 16
                        vw_ = vw_0
                        sy.dma_start(out=qz_sb[:, 0:vw_],
                                     in_=ap["qzg"][:, fr0:fr0 + vw_]
                                     ).then_inc(s_p5l.h, 16)
                        s_p5l.v += 16
                        sy.dma_start(out=qz_sb[:, 512:512 + vw_],
                                     in_=ap["qzg"][:, FRW + fr0:FRW + fr0 + vw_]
                                     ).then_inc(s_p5l.h, 16)
                        s_p5l.v += 16

                    s_ddb = sm["s_ddb"]
                    @blk.gpsimd
                    def _(g, fr0=fr0, frw=frw, n=n, first=first_group,
                          DQB_prev=DQB_prev):
                        if not first:
                            g.wait_ge(s_p5v.h, DQB_prev)
                        for wc in range(4):
                            g.dma_start(
                                out=ag_sb[:, 512 * wc:512 * wc + frw],
                                in_=ap["s_slot"][8 * wc:8 * (wc + 1),
                                                 n * FRP + fr0:n * FRP + fr0 + frw]
                                .unsqueeze(1).broadcast_to([8, 16, frw])
                            ).then_inc(s_ddb.h, 16)
                            s_ddb.v += 16
                            g.dma_start(
                                out=au_sb[:, 512 * wc:512 * wc + frw],
                                in_=ap["s_slot"][8 * wc:8 * (wc + 1),
                                                 IPAD + n * FRP + fr0:IPAD + n * FRP + fr0 + frw]
                                .unsqueeze(1).broadcast_to([8, 16, frw])
                            ).then_inc(s_ddb.h, 16)
                            s_ddb.v += 16
                    LD = s_p5l.v
                    LDB = s_ddb.v

                    @blk.vector
                    def _(v, fr0=fr0, frw=frw, vw=vw, n=n, LD=LD, LDB=LDB,
                          MM_PREV=MM_PREV, bg_sb=bg_sb):
                        v.wait_ge(s_p5l.h, LD)
                        v.wait_ge(sm["s_ddb"].h, LDB)
                        if MM_PREV > 0:
                            v.wait_ge(s_p5p.h, MM_PREV - 1)
                        if vw < frw:
                            v.memset(bg_sb[:, 0:frw], 0.0)
                            v.memset(bg_sb[:, FRP:FRP + frw], 0.0)
                            v.drain()
                        v.tensor_scalar(zrow[:, 0:vw], qz_sb[:, 0:vw],
                                        28 - 4 * n, 28,
                                        OP.logical_shift_left,
                                        OP.logical_shift_right)
                        v.drain()
                        v.scalar_tensor_tensor(bg_sb[:, 0:vw], zrow[:, 0:vw],
                                               1.0, srow[:, 0:vw],
                                               OP.add, OP.mult)
                        v.drain()
                        v.tensor_scalar(zrow[:, 0:vw], qz_sb[:, 512:512 + vw],
                                        28 - 4 * n, 28,
                                        OP.logical_shift_left,
                                        OP.logical_shift_right)
                        v.drain()
                        v.scalar_tensor_tensor(bg_sb[:, FRP:FRP + vw],
                                               zrow[:, 0:vw],
                                               1.0, srow[:, 512:512 + vw],
                                               OP.add, OP.mult)
                        v.drain()
                        for hc in range(16):
                            nh, wc = hc // 4, hc % 4
                            v.tensor_scalar(nib5[:, 0:frw],
                                            qg_sb[:, 512 * wc:512 * wc + frw],
                                            28 - 4 * nh, 28,
                                            OP.logical_shift_left,
                                            OP.logical_shift_right)
                            v.tensor_scalar(nib5[:, 512:512 + frw],
                                            qu_sb[:, 512 * wc:512 * wc + frw],
                                            28 - 4 * nh, 28,
                                            OP.logical_shift_left,
                                            OP.logical_shift_right)
                            v.drain()
                            v.tensor_tensor(wg_sb[:, 512 * hc:512 * hc + frw],
                                            nib5[:, 0:frw],
                                            ag_sb[:, 512 * wc:512 * wc + frw],
                                            OP.mult)
                            v.tensor_tensor(wu_sb[:, 512 * hc:512 * hc + frw],
                                            nib5[:, 512:512 + frw],
                                            au_sb[:, 512 * wc:512 * wc + frw],
                                            OP.mult)
                            v.drain()
                        v.drain().then_inc(s_p5v.h, 1)
                        s_p5v.v += 1
                    DQA = s_p5v.v

                    @blk.vector
                    def _(v, frw=frw, MM_PREV=MM_PREV):
                        if MM_PREV > 0:
                            v.wait_ge(s_p5p.h, MM_PREV)
                        for hc in range(16, 32):
                            nh, wc = hc // 4, hc % 4
                            v.tensor_scalar(nib5[:, 0:frw],
                                            qg_sb[:, 512 * wc:512 * wc + frw],
                                            28 - 4 * nh, 28,
                                            OP.logical_shift_left,
                                            OP.logical_shift_right)
                            v.tensor_scalar(nib5[:, 512:512 + frw],
                                            qu_sb[:, 512 * wc:512 * wc + frw],
                                            28 - 4 * nh, 28,
                                            OP.logical_shift_left,
                                            OP.logical_shift_right)
                            v.drain()
                            v.tensor_tensor(wg_sb[:, 512 * hc:512 * hc + frw],
                                            nib5[:, 0:frw],
                                            ag_sb[:, 512 * wc:512 * wc + frw],
                                            OP.mult)
                            v.tensor_tensor(wu_sb[:, 512 * hc:512 * hc + frw],
                                            nib5[:, 512:512 + frw],
                                            au_sb[:, 512 * wc:512 * wc + frw],
                                            OP.mult)
                            v.drain()
                        v.drain().then_inc(s_p5v.h, 1)
                        s_p5v.v += 1
                    DQ = s_p5v.v
                    DQB_prev = DQ
                    if dbg and (dbg["mask"] & 2) and fr0 == 0 and n == 0:
                        @blk.sync
                        def _(sy, DQ=DQ):
                            sy.wait_ge(s_p5v.h, DQ)
                            for nm, src_ap in [
                                    ("d_x1tp", x1tp[:, 0:TPC]),
                                    ("d_wg", wg_sb[:, 0:512]),
                                    ("d_ag", ag_sb[:, 0:512]),
                                    ("d_xg", xgT[:, :]),
                                    ("d_bg", bg_sb[:, :])]:
                                sy.dma_start(out=dbg[nm][:, :] if dbg[nm].ndim == 2 else dbg[nm],
                                             in_=src_ap).then_inc(s_out.h, 16)
                                s_out.v += 16

                    nchunk = (frw + 127) // 128
                    for ci in range(nchunk):
                        fpc = (n * FRP + fr0) // 128 + ci
                        for th in range(2):
                            gbank = ps5[0 if th == 0 else 2]
                            ubank = ps5[1 if th == 0 else 3]
                            EV_PREV = pair_done[th]
                            @blk.tensor
                            def _(t, ci=ci, th=th, gbank=gbank, ubank=ubank,
                                  DQA=DQA, EV_PREV=EV_PREV):
                                t.wait_ge(s_p5v.h, DQA)
                                if EV_PREV is not None:
                                    t.wait_ge(s_p5s.h, EV_PREV[0])
                                    t.wait_ge(s_p5v.h, EV_PREV[1])
                                for hc in range(16):
                                    t.matmul(
                                        gbank[:, :],
                                        wg_sb[:, 512 * hc + 128 * ci:512 * hc + 128 * (ci + 1)],
                                        x1tp[:, TPC * hc + 512 * th:TPC * hc + 512 * (th + 1)],
                                        start=(hc == 0), stop=False)
                                    mmi = t.matmul(
                                        ubank[:, :],
                                        wu_sb[:, 512 * hc + 128 * ci:512 * hc + 128 * (ci + 1)],
                                        x1tp[:, TPC * hc + 512 * th:TPC * hc + 512 * (th + 1)],
                                        start=(hc == 0), stop=False)
                                mmi.then_inc(s_p5p.h, 1)
                                s_p5p.v += 1

                            @blk.tensor
                            def _(t, ci=ci, th=th, gbank=gbank, ubank=ubank,
                                  DQ=DQ, bg_sb=bg_sb):
                                t.wait_ge(s_p5v.h, DQ)
                                for hc in range(16, 32):
                                    t.matmul(
                                        gbank[:, :],
                                        wg_sb[:, 512 * hc + 128 * ci:512 * hc + 128 * (ci + 1)],
                                        x1tp[:, TPC * hc + 512 * th:TPC * hc + 512 * (th + 1)],
                                        start=False, stop=False)
                                    t.matmul(
                                        ubank[:, :],
                                        wu_sb[:, 512 * hc + 128 * ci:512 * hc + 128 * (ci + 1)],
                                        x1tp[:, TPC * hc + 512 * th:TPC * hc + 512 * (th + 1)],
                                        start=False, stop=False)
                                t.matmul(gbank[:, :],
                                         bg_sb[:, 128 * ci:128 * (ci + 1)],
                                         xgT[:, 512 * th:512 * (th + 1)],
                                         start=False, stop=False)
                                t.matmul(gbank[:, :],
                                         bg_sb[:, 128 * ci:128 * (ci + 1)],
                                         xgL[:, 512 * th:512 * (th + 1)],
                                         start=False, stop=True)
                                t.matmul(ubank[:, :],
                                         bg_sb[:, FRP + 128 * ci:FRP + 128 * (ci + 1)],
                                         xgT[:, 512 * th:512 * (th + 1)],
                                         start=False, stop=False)
                                t.matmul(ubank[:, :],
                                         bg_sb[:, FRP + 128 * ci:FRP + 128 * (ci + 1)],
                                         xgL[:, 512 * th:512 * (th + 1)],
                                         start=False, stop=True).then_inc(s_p5p.h, 1)
                                s_p5p.v += 1
                            MM = s_p5p.v

                            @blk.scalar
                            def _(sc, gbank=gbank, MM=MM, prev=sg_used):
                                sc.wait_ge(s_p5p.h, MM)
                                if prev is not None:
                                    sc.wait_ge(s_p5v.h, prev)
                                sc.activation(sgate[:, :], gbank[:, :],
                                              AF.Silu).then_inc(s_p5s.h, 1)
                                s_p5s.v += 1
                            SG = s_p5s.v

                            @blk.vector
                            def _(v, th=th, ubank=ubank, SG=SG):
                                v.wait_ge(s_p5s.h, SG)
                                if s_hst.v > 0:
                                    v.wait_ge(s_hst.h, s_hst.v)
                                v.tensor_tensor(h_sb[:, 512 * th:512 * (th + 1)],
                                                sgate[:, :], ubank[:, :],
                                                OP.mult)
                                v.drain().then_inc(s_p5v.h, 1)
                                s_p5v.v += 1
                            pair_done[th] = (SG, s_p5v.v)
                            sg_used = s_p5v.v

                            @blk.sync
                            def _(sy, fpc=fpc, th=th, HT=s_p5v.v):
                                sy.wait_ge(s_p5v.h, HT)
                                sy.dma_start(
                                    out=hdram[fpc, :, 512 * th:512 * (th + 1)],
                                    in_=h_sb[:, 512 * th:512 * (th + 1)]
                                ).then_inc(s_hst.h, 16)
                                s_hst.v += 16
                    first_group = False

        stP45.__exit__(None, None, None)
        _barrier(blk, s_bar)

        if dbg and (dbg["mask"] & 4):
            @blk.sync
            def _(sy):
                sy.wait_ge(s_p4.h, s_p4.v)
                sy.wait_ge(s_st.h, s_st.v)
                sy.dma_start(out=dbg["d_x1"][:, :], in_=x1d[:, :]
                             ).then_inc(s_out.h, 16)
                s_out.v += 16
                sy.wait_ge(s_hst.h, s_hst.v)
                sy.dma_start(out=dbg["d_h"][0, :, :], in_=hdram[0, :, :]
                             ).then_inc(s_out.h, 16)
                s_out.v += 16
                sy.dma_start(out=dbg["d_h"][1, :, :], in_=hdram[87, :, :]
                             ).then_inc(s_out.h, 16)
                s_out.v += 16
                sy.dma_start(out=dbg["d_hfull"][:, :, :], in_=hdram[:, :, :]
                             ).then_inc(s_out.h, 16)
                s_out.v += 16
                sy.wait_ge(sm["s_wdd"].h, sm["s_wdd"].v)
                sy.dma_start(out=dbg["d_wdd"][0, :, :], in_=wdd[0, :, :]
                             ).then_inc(s_out.h, 16)
                s_out.v += 16
                sy.dma_start(out=dbg["d_wdd"][1, :, :], in_=wdd[87, :, :]
                             ).then_inc(s_out.h, 16)
                s_out.v += 16

        if STOP_PHASE < 8:
            _dummy_out(nc, blk, y, s_out)
            return

        st7 = contextlib.ExitStack()
        with st7:
            ec7 = st7.enter_context
            hh = ec7(nc.sbuf_tensor("hh", [128, NFPC * 512], BF16))
            wdt6 = [ec7(nc.sbuf_tensor(f"wdt6_{i}", [128, 512], BF16))
                    for i in range(3)]
            bd_sb = ec7(nc.sbuf_tensor("bd_sb", [88, 512], BF16))
            zd_sb = ec7(nc.sbuf_tensor("zd_sb", [88, 512], U32))
            sd_sb = ec7(nc.sbuf_tensor("sd_sb", [88, 512], F32))
            qzd_sb = ec7(nc.sbuf_tensor("qzd_sb", [88, 512], U32))
            hgT = ec7(nc.sbuf_tensor("hgT", [88, 512], BF16))
            hgL = ec7(nc.sbuf_tensor("hgL", [88, 512], BF16))
            hgF = ec7(nc.sbuf_tensor("hgF", [88, 512], F32))
            yf_sb = ec7(nc.sbuf_tensor("yf_sb", [128, 32 * 512], BF16))
            ytok = ec7(nc.sbuf_tensor("ytok", [128, DM], BF16))
            x1r = ec7(nc.sbuf_tensor("x1r", [128, DM], F32))
            yout = ec7(nc.sbuf_tensor("yout", [128, DM], F32))

            ps6 = [ec7(nc.psum_tensor(f"ps6_{i}", [128, 512], F32))
                   for i in range(4)]
            psh = ec7(nc.psum_tensor("psh", [88, 512], F32))
            psb6 = [ec7(nc.psum_tensor(f"psb6_{i}", [128, 512], BF16))
                    for i in range(2)]

            s_p6h = sm["s_p6h"]; s_p6sd = sm["s_p6sd"]; s_p6x = sm["s_p6x"]
            wsems = [sm["s_p6w0"], sm["s_p6w1"], sm["s_p6w2"]]
            @blk.sync
            def _(sy):
                sy.wait_ge(s_hst.h, s_hst.v)
                sy.dma_start(out=qzd_sb[0:NGD, :], in_=ap["qzd"][:, :]
                             ).then_inc(s_p6sd.h, 16)
                s_p6sd.v += 16

            for th in range(2):
                MM_PREV_TH = s_p6p.v
                @blk.sync
                def _(sy, th=th, MM_PREV_TH=MM_PREV_TH):
                    if th == 1:
                        sy.wait_ge(s_p6p.h, MM_PREV_TH)
                    for c in range(NFPC):
                        sy.dma_start(out=hh[:, 512 * c:512 * (c + 1)],
                                     in_=hdram[c, :, 512 * th:512 * (th + 1)]
                                     ).then_inc(s_p6h.h, 16)
                        s_p6h.v += 16
                H_LD = s_p6h.v

                @blk.tensor
                def _(t, H_LD=H_LD):
                    t.wait_ge(s_p6h.h, H_LD)
                    for c in range(11):
                        for n in range(8):
                            fpc = n * 11 + c
                            mmi = t.matmul(psh[:, :], donesb[:, 88 * c:88 * (c + 1)],
                                     hh[:, 512 * fpc:512 * (fpc + 1)],
                                     start=(c == 0 and n == 0),
                                     stop=(c == 10 and n == 7))
                    mmi.then_inc(s_p6p.h, 1)
                    s_p6p.v += 1
                HG = s_p6p.v

                @blk.vector
                def _(v, HG=HG):
                    v.wait_ge(s_p6p.h, HG)
                    v.tensor_scalar(hgF[:, :], psh[:, :], -1.0, None, OP.mult)
                    v.drain()
                    v.tensor_copy(hgT[:, :], hgF[:, :])
                    v.drain()
                    v.tensor_tensor(hgF[:, :], hgF[:, :], hgT[:, :], OP.subtract)
                    v.drain()
                    v.tensor_copy(hgL[:, :], hgF[:, :])
                    v.drain().then_inc(s_p6v.h, 1)
                    s_p6v.v += 1

                for ht in range(8):
                    MM_PREV = s_p6p.v
                    @blk.sync
                    def _(sy, ht=ht, MM_PREV=MM_PREV):
                        sy.wait_ge(s_p6v.h, s_p6v.v)
                        sy.dma_start(out=sd_sb[0:NGD, :],
                                     in_=ap["sdown"][:, 512 * ht:512 * (ht + 1)]
                                     ).then_inc(s_p6sd.h, 16)
                        s_p6sd.v += 16
                    SD_LD = s_p6sd.v

                    @blk.vector
                    def _(v, ht=ht, SD_LD=SD_LD, MM_PREV=MM_PREV):
                        v.wait_ge(s_p6sd.h, SD_LD)
                        v.wait_ge(s_p6p.h, MM_PREV)
                        v.memset(bd_sb[:, :], 0.0)
                        v.drain()
                        for n in range(8):
                            v.tensor_scalar(
                                zd_sb[0:NGD, n:512:8],
                                qzd_sb[0:NGD, 64 * ht:64 * (ht + 1)],
                                28 - 4 * n, 28,
                                OP.logical_shift_left, OP.logical_shift_right)
                        v.drain()
                        v.scalar_tensor_tensor(bd_sb[0:NGD, :], zd_sb[0:NGD, :],
                                               1.0, sd_sb[0:NGD, :],
                                               OP.add, OP.mult)
                        v.drain().then_inc(s_p6v.h, 1)
                        s_p6v.v += 1
                    BD = s_p6v.v
                    if dbg and (dbg["mask"] & 8) and th == 0 and ht == 0:
                        @blk.sync
                        def _(sy, BD=BD):
                            sy.wait_ge(s_p6v.h, BD)
                            sy.dma_start(out=dbg["d_bd"][:, :], in_=bd_sb[:, :]
                                         ).then_inc(s_out.h, 16)
                            s_out.v += 16
                            sy.dma_start(out=dbg["d_zd"][:, :], in_=zd_sb[:, :]
                                         ).then_inc(s_out.h, 16)
                            s_out.v += 16
                            sy.dma_start(out=dbg["d_hg"][:, :], in_=hgT[:, :]
                                         ).then_inc(s_out.h, 16)
                            s_out.v += 16

                    for c in range(NFPC):
                        slot = wdt6[c % 3]
                        wsem = wsems[c % 3]
                        @blk.sync
                        def _(sy, c=c, ht=ht, th=th, slot=slot,
                              MM_PREV=MM_PREV):
                            if c >= 3:
                                sy.wait_ge(s_p6p.h, s_p6p.v - 2)
                            elif ht > 0 or th > 0:
                                sy.wait_ge(s_p6p.h, MM_PREV)
                            sy.dma_start(out=slot[:, :],
                                         in_=wdd[c, :, 512 * ht:512 * (ht + 1)]
                                         ).then_inc(wsem.h, 16)
                            wsem.v += 16
                        WD = wsem.v

                        EV_PREV = s_p6s.v
                        @blk.tensor
                        def _(t, c=c, slot=slot, WD=WD, BD=BD, wsem=wsem,
                              EV_PREV=EV_PREV):
                            t.wait_ge(wsem.h, WD)
                            if c == 0:
                                t.wait_ge(s_p6v.h, BD)
                                if EV_PREV > 0:
                                    t.wait_ge(s_p6s.h, EV_PREV)
                            for m in range(4):
                                mmi = t.matmul(ps6[m][:, :],
                                         slot[:, 128 * m:128 * (m + 1)],
                                         hh[:, 512 * c:512 * (c + 1)],
                                         start=(c == 0), stop=False)
                            if c == NFPC - 1 and os.environ.get("KCOR", "1") == "1":
                                for m in range(4):
                                    mmi = t.matmul(ps6[m][:, :],
                                             bd_sb[:, 128 * m:128 * (m + 1)],
                                             hgT[:, :],
                                             start=False, stop=False)
                                    mmi = t.matmul(ps6[m][:, :],
                                             bd_sb[:, 128 * m:128 * (m + 1)],
                                             hgL[:, :],
                                             start=False, stop=True)
                            elif c == NFPC - 1:
                                for m in range(4):
                                    mmi = t.matmul(ps6[m][:, :],
                                             slot[:, 128 * m:128 * (m + 1)],
                                             hh[:, 512 * c:512 * (c + 1)],
                                             start=False, stop=True)
                            mmi.then_inc(s_p6p.h, 1)
                            s_p6p.v += 1
                    DONE = s_p6p.v

                    @blk.scalar
                    def _(sc, ht=ht, DONE=DONE):
                        sc.wait_ge(s_p6p.h, DONE)
                        for m in range(4):
                            mm = 4 * ht + m
                            sc.copy(yf_sb[:, 512 * mm:512 * (mm + 1)],
                                    ps6[m][:, :])
                        sc.drain().then_inc(s_p6s.h, 1)
                        s_p6s.v += 1

                YF = s_p6s.v
                if dbg and (dbg["mask"] & 16):
                    @blk.sync
                    def _(sy, th=th, YF=YF):
                        sy.wait_ge(s_p6s.h, YF)
                        sy.dma_start(out=dbg["d_yf"][th, :, :],
                                     in_=yf_sb[:, 0:512]).then_inc(s_out.h, 16)
                        s_out.v += 16
                for tc in range(4):
                    trow = 512 * th + 128 * tc
                    @blk.sync
                    def _(sy, trow=trow):
                        if s_out.v > 0:
                            sy.wait_ge(s_out.h, s_out.v)
                        sy.dma_start(out=x1r[:, :],
                                     in_=x1d[trow:trow + 128, :]
                                     ).then_inc(s_p6x.h, 16)
                        s_p6x.v += 16
                    XL = s_p6x.v

                    ytok_done = s_p6v.v
                    cp_marks = [None, None]
                    for mm in range(32):
                        pb6 = psb6[mm % 2]
                        @blk.tensor
                        def _(t, tc=tc, mm=mm, YF=YF, pb6=pb6,
                              cp=cp_marks[mm % 2], yd=ytok_done):
                            t.wait_ge(s_p6s.h, YF)
                            if mm == 0:
                                t.wait_ge(s_p6v.h, yd)
                            if cp is not None:
                                t.wait_ge(s_p6v.h, cp)
                            t.transpose(
                                pb6[:, 0:128],
                                yf_sb[:, 512 * mm + 128 * tc:512 * mm + 128 * tc + 128],
                                ident[:, :]).then_inc(s_p6p.h, 1)
                            s_p6p.v += 1

                        @blk.vector
                        def _(v, mm=mm, pb6=pb6, TRM=s_p6p.v):
                            v.wait_ge(s_p6p.h, TRM)
                            v.tensor_copy(ytok[:, 128 * mm:128 * (mm + 1)],
                                          pb6[:, 0:128]
                                          ).then_inc(s_p6v.h, 1)
                            s_p6v.v += 1
                        cp_marks[mm % 2] = s_p6v.v

                    @blk.vector
                    def _(v, XL=XL, CP=s_p6v.v):
                        v.wait_ge(s_p6v.h, CP)
                        v.wait_ge(s_p6x.h, XL)
                        v.tensor_tensor(x1r[:, :], x1r[:, :], ytok[:, :], OP.add)
                        v.drain()
                        for j in range(8):
                            v.bn_stats(stats[:, 6 * j:6 * (j + 1)],
                                       x1r[:, 512 * j:512 * (j + 1)])
                        v.drain()
                        v.bn_aggr(mv[:, :], stats[:, 0:48])
                        v.drain()
                        v.tensor_scalar(veps[:, :], mv[:, 1:2], EPS, None, OP.add)
                        v.drain().then_inc(s_p6v.h, 1)
                        s_p6v.v += 1

                    @blk.scalar
                    def _(sc):
                        sc.wait_ge(s_p6v.h, s_p6v.v)
                        sc.activation(stdv[:, :], veps[:, :], AF.Sqrt
                                      ).then_inc(s_p6sq.h, 1)
                        s_p6sq.v += 1

                    @blk.vector
                    def _(v, SQ=s_p6sq.v):
                        v.wait_ge(s_p6sq.h, SQ)
                        v.reciprocal(rstd[:, :], stdv[:, :])
                        v.drain()
                        v.tensor_scalar(yout[:, :], x1r[:, :], mv[:, 0:1],
                                        rstd[:, :], OP.subtract, OP.mult)
                        v.drain().then_inc(s_p6v.h, 1)
                        s_p6v.v += 1

                    @blk.sync
                    def _(sy, trow=trow):
                        sy.wait_ge(s_p6v.h, s_p6v.v)
                        sy.dma_start(out=y[trow:trow + 128, :],
                                     in_=yout[:, :]).then_inc(s_out.h, 16)
                        s_out.v += 16

            @blk.sync
            def _(sy):
                sy.wait_ge(s_out.h, s_out.v)


def _dummy_out(nc, blk, y, s_out, wait=()):
    with nc.sbuf_tensor("dmy_out", [128, DM], F32) as dummy:
        @blk.vector
        def _(v):
            v.memset(dummy[:, :], 0.0)
            v.drain().then_inc(s_out.h, 1)
            s_out.v += 1

        @blk.sync
        def _(sy):
            for h, val in wait:
                sy.wait_ge(h, val)
            sy.wait_ge(s_out.h, s_out.v)
            for tt in range(8):
                sy.dma_start(out=y[128 * tt:128 * (tt + 1), :],
                             in_=dummy[:, :]).then_inc(s_out.h, 16)
                s_out.v += 16
            sy.wait_ge(s_out.h, s_out.v)


def _qkv(nc, blk, ap, b, wkv, xst, ropes, q_sb, k_sb, vtok, qraw, rt1, rt2,
         ps, psb, ident, prot, sm, CONST):
    s_wkv = sm["s_wkv"]; s_x = sm["s_x"]; s_rope = sm["s_rope"]
    s_qkps = sm["s_qkps"]; s_ev = sm["s_ev"]; s_rp = sm["s_rp"]
    s_const = sm["s_const"]; s_wo = sm["s_wo"]; s_sc = sm["s_sc"]
    s_ve2 = sm["s_ve2"]
    x_sems = [s_x, sm["s_x2"], sm["s_x3"], sm["s_x4"]]

    @blk.gpsimd
    def _(g, b=b):
        if b == 1:
            g.wait_ge(s_qkps.h, s_qkps.v)
            g.wait_ge(s_wo.h, s_wo.v)
        for hc in range(32):
            g.dma_start(out=wkv[:, 768 * hc:768 * (hc + 1)],
                        in_=ap["wkvT"][128 * hc:128 * (hc + 1), :]
                        ).then_inc(s_wkv.h, 16)
            s_wkv.v += 16
    WKV = s_wkv.v

    rp_slot = [0] * 6

    for st in range(8):
        t0 = b * S + 512 * st
        p0 = 512 * st
        @blk.gpsimd
        def _(g, t0=t0, st=st, b=b):
            if st > 0 or b == 1:
                g.wait_ge(s_qkps.h, s_qkps.v)
            for grp in range(4):
                for hc in range(8 * grp, 8 * grp + 8):
                    g.dma_start(out=xst[:, 512 * hc:512 * (hc + 1)],
                                in_=ap["xT"][128 * hc:128 * (hc + 1), t0:t0 + 512]
                                ).then_inc(x_sems[grp].h, 16)
                    x_sems[grp].v += 16
        XG = [x_sems[g_].v for g_ in range(4)]

        @blk.sync
        def _(sy, p0=p0, st=st, b=b):
            if st > 0 or b == 1:
                sy.wait_ge(s_rp.h, s_rp.v)
            sy.dma_start(out=ropes[:, 0:512],
                         in_=ap["cos_t"][:, p0:p0 + 512]).then_inc(s_rope.h, 16)
            s_rope.v += 16
            sy.dma_start(out=ropes[:, 512:1024],
                         in_=ap["sin_t"][:, p0:p0 + 512]).then_inc(s_rope.h, 16)
            s_rope.v += 16
        RP = s_rope.v

        if KQKV < 2:
            continue
        @blk.tensor
        def _(t, st=st, b=b, XG=XG):
            t.wait_ge(s_wkv.h, WKV)
            if st > 0 or b == 1:
                t.wait_ge(s_ev.h, s_ev.v)
                t.wait_ge(s_rp.h, s_rp.v)
            if b == 1 and st == 0:
                t.wait_ge(s_sc.h, s_sc.v)
                t.wait_ge(s_ve2.h, s_ve2.v)
                t.wait_ge(sm["s_woe"].h, sm["s_woe"].v)
            for oc in range(6):
                for hc in range(32):
                    if oc == 0 and hc % 8 == 0:
                        t.wait_ge(x_sems[hc // 8].h, XG[hc // 8])
                    mmi = t.matmul(ps[oc][:, :],
                             wkv[:, 768 * hc + 128 * oc:768 * hc + 128 * (oc + 1)],
                             xst[:, 512 * hc:512 * (hc + 1)],
                             start=(hc == 0), stop=(hc == 31))
                mmi.then_inc(s_qkps.h, 1)
                s_qkps.v += 1
        QKB = s_qkps.v - 6

        if KQKV < 3:
            continue
        for oc in range(6):
            if os.environ.get("KEVAC", "s") == "v":
                @blk.vector
                def _(v, oc=oc, QKB=QKB, rp_need=rp_slot[oc]):
                    v.wait_ge(s_qkps.h, QKB + oc + 1)
                    if rp_need > 0:
                        v.wait_ge(s_rp.h, rp_need)
                    v.tensor_copy(qraw[:, 512 * oc:512 * (oc + 1)], ps[oc][:, :])
                    v.drain().then_inc(s_ev.h, 1)
                    s_ev.v += 1
            else:
                @blk.scalar
                def _(sc, oc=oc, QKB=QKB, rp_need=rp_slot[oc]):
                    sc.wait_ge(s_qkps.h, QKB + oc + 1)
                    if rp_need > 0:
                        sc.wait_ge(s_rp.h, rp_need)
                    sc.copy(qraw[:, 512 * oc:512 * (oc + 1)], ps[oc][:, :])
                    sc.drain().then_inc(s_ev.h, 1)
                    s_ev.v += 1
            EV = s_ev.v

            if KQKV < 4 or (oc == 5 and KQKV < 5):
                continue
            if oc < 5:
                @blk.tensor
                def _(t, oc=oc, EV=EV):
                    t.wait_ge(s_ev.h, EV)
                    t.wait_ge(s_const.h, CONST)
                    t.matmul(ps[oc][:, :], prot[:, :],
                             qraw[:, 512 * oc:512 * (oc + 1)],
                             start=True, stop=True).then_inc(s_qkps.h, 1)
                    s_qkps.v += 1
                PR = s_qkps.v

                dst = (q_sb[:, S * oc + 512 * st:S * oc + 512 * (st + 1)]
                       if oc < 4 else k_sb[:, 512 * st:512 * (st + 1)])
                @blk.vector
                def _(v, oc=oc, dst=dst, PR=PR, RP=RP):
                    v.wait_ge(s_qkps.h, PR)
                    v.wait_ge(s_rope.h, RP)
                    v.tensor_tensor(rt1[:, :], qraw[:, 512 * oc:512 * (oc + 1)],
                                    ropes[:, 0:512], OP.mult)
                    v.tensor_tensor(rt2[:, :], ps[oc][:, :],
                                    ropes[:, 512:1024], OP.mult)
                    v.drain()
                    v.tensor_tensor(dst, rt1[:, :], rt2[:, :], OP.add)
                    v.drain().then_inc(s_rp.h, 1)
                    s_rp.v += 1
                rp_slot[oc] = s_rp.v
            else:
                @blk.tensor
                def _(t, st=st, EV=EV):
                    t.wait_ge(s_ev.h, EV)
                    t.wait_ge(s_const.h, CONST)
                    if s_rp.v > 0:
                        t.wait_ge(s_rp.h, s_rp.v)
                    for kc in range(4):
                        tri = t.transpose(psb[:, 128 * kc:128 * (kc + 1)],
                                    qraw[:, 512 * 5 + 128 * kc:512 * 5 + 128 * (kc + 1)],
                                    ident[:, :])
                    tri.then_inc(s_qkps.h, 1)
                    s_qkps.v += 1
                VT = s_qkps.v

                @blk.vector
                def _(v, st=st, VT=VT):
                    v.wait_ge(s_qkps.h, VT)
                    v.tensor_copy(vtok[:, 512 * st:512 * (st + 1)], psb[:, :])
                    v.drain().then_inc(s_rp.h, 1)
                    s_rp.v += 1
                rp_slot[5] = s_rp.v


def _attn(nc, blk, b, q_sb, k_sb, vtok, outT, scrs, prb, prT, maskb,
          rmax, nmax, ssum, rsum, ps, psbp, ident, sm):
    s_pe2 = sm["s_pe2"]; s_sc = sm["s_sc"]; s_ve2 = sm["s_ve2"]
    s_rp = sm["s_rp"]; s_qkps = sm["s_qkps"]

    s_msk = sm["s_msk"]
    MASK_MARK = getattr(s_msk, "mask_mark", 0)
    qk_banks = [0, 1, 3, 4]
    bank_evac = [None, None, None, None]
    bank_i = 0
    first_diag = [True]
    RP_ALL = s_rp.v

    exp_hist = getattr(s_sc, "attn_exp_hist", [])
    qti = [getattr(s_sc, "attn_qti", 0)]
    for h in range(4):
        for qg in range(8):
            ktiles = qg + 1
            pe_group0 = s_pe2.v
            sc_group0 = s_sc.v
            for qt in range(4):
                q0 = 512 * qg + 128 * qt
                scr = scrs[qti[0] % 2]
                qti[0] += 1
                for kt in range(ktiles):
                    bi = bank_i % 4
                    bank_i += 1
                    bank = ps[qk_banks[bi]]
                    need = bank_evac[bi]
                    @blk.tensor
                    def _(t, h=h, q0=q0, kt=kt, bank=bank, need=need,
                          first=(h == 0 and qg == 0 and qt == 0 and kt == 0)):
                        if first:
                            t.wait_ge(s_rp.h, RP_ALL)
                        if need is not None:
                            t.wait_ge(s_ve2.h if need[0] else s_sc.h, need[1])
                        t.matmul(bank[:, :],
                                 q_sb[:, S * h + q0:S * h + q0 + 128],
                                 k_sb[:, 512 * kt:512 * (kt + 1)],
                                 start=True, stop=True).then_inc(s_pe2.h, 1)
                        s_pe2.v += 1
                    PE = s_pe2.v

                    diag = (kt == ktiles - 1)
                    prev_exp = exp_hist[-2] if len(exp_hist) >= 2 else 0
                    if diag:
                        @blk.vector
                        def _(v, qt=qt, kt=kt, bank=bank, PE=PE, scr=scr,
                              prev_exp=prev_exp, fd=first_diag[0]):
                            if fd:
                                v.wait_ge(s_msk.h, MASK_MARK)
                            v.wait_ge(s_pe2.h, PE)
                            if prev_exp > 0:
                                v.wait_ge(s_sc.h, prev_exp)
                            v.tensor_tensor(scr[:, 512 * kt:512 * (kt + 1)],
                                            bank[:, :],
                                            maskb[:, 512 * qt:512 * (qt + 1)],
                                            OP.add)
                            v.drain().then_inc(s_ve2.h, 1)
                            s_ve2.v += 1
                        bank_evac[bi] = (True, s_ve2.v)
                        first_diag[0] = False
                    else:
                        @blk.scalar
                        def _(sc, kt=kt, bank=bank, PE=PE, scr=scr,
                              prev_exp=prev_exp):
                            sc.wait_ge(s_pe2.h, PE)
                            if prev_exp > 0:
                                sc.wait_ge(s_sc.h, prev_exp)
                            sc.copy(scr[:, 512 * kt:512 * (kt + 1)], bank[:, :])
                            sc.drain().then_inc(s_sc.h, 1)
                            s_sc.v += 1
                        bank_evac[bi] = (False, s_sc.v)

                EVAC_S = s_sc.v
                EVAC_V = s_ve2.v
                kw = 512 * ktiles
                @blk.vector
                def _(v, qt=qt, kw=kw, EVAC_S=EVAC_S, EVAC_V=EVAC_V, scr=scr):
                    v.wait_ge(s_sc.h, EVAC_S)
                    v.wait_ge(s_ve2.h, EVAC_V)
                    v.reduce_max(rmax[:, qt:qt + 1], scr[:, 0:kw], axis=AXX)
                    v.drain()
                    v.tensor_scalar(nmax[:, qt:qt + 1], rmax[:, qt:qt + 1],
                                    -INV_SQRT_HD, None, OP.mult)
                    v.drain().then_inc(s_ve2.h, 1)
                    s_ve2.v += 1
                NM = s_ve2.v

                @blk.scalar
                def _(sc, qt=qt, kw=kw, NM=NM, pe_group0=pe_group0, scr=scr):
                    sc.wait_ge(s_ve2.h, NM)
                    sc.wait_ge(s_pe2.h, pe_group0)
                    sc.activation(prb[:, S * qt:S * qt + kw],
                                  scr[:, 0:kw],
                                  AF.Exp, bias=nmax[:, qt:qt + 1],
                                  scale=np.float32(INV_SQRT_HD),
                                  accum_out=ssum[:, qt:qt + 1]
                                  ).then_inc(s_sc.h, 1)
                    s_sc.v += 1
                exp_hist.append(s_sc.v)
                EX = s_sc.v

                @blk.vector
                def _(v, qt=qt, kw=kw, EX=EX):
                    v.wait_ge(s_sc.h, EX)
                    v.reciprocal(rsum[:, qt:qt + 1], ssum[:, qt:qt + 1])
                    v.drain()
                    v.tensor_scalar(prb[:, S * qt:S * qt + kw],
                                    prb[:, S * qt:S * qt + kw],
                                    rsum[:, qt:qt + 1], None, OP.mult)
                    v.drain().then_inc(s_ve2.h, 1)
                    s_ve2.v += 1
            PROBS = s_ve2.v

            if int(os.environ.get("KATT", "9")) < 3:
                continue
            av_bank = ps[2]
            nkc = 4 * ktiles
            for kc in range(nkc):
                pb = psbp[kc % 2]
                half = (kc % 2) * 512
                @blk.tensor
                def _(t, kc=kc, pb=pb, PROBS=PROBS, sc_group0=sc_group0):
                    if kc == 0:
                        t.wait_ge(s_ve2.h, PROBS)
                        if sc_group0 > 0:
                            t.wait_ge(s_sc.h, sc_group0)
                    if kc >= 2:
                        t.wait_ge(s_ve2.h, PROBS + kc - 1)
                    for qt in range(4):
                        tri = t.transpose(pb[:, 128 * qt:128 * (qt + 1)],
                                    prb[:, S * qt + 128 * kc:S * qt + 128 * (kc + 1)],
                                    ident[:, :])
                    tri.then_inc(s_pe2.h, 1)
                    s_pe2.v += 1
                TR = s_pe2.v

                @blk.vector
                def _(v, half=half, pb=pb, TR=TR):
                    v.wait_ge(s_pe2.h, TR)
                    v.tensor_copy(prT[:, half:half + 512],
                                  pb[:, :]).then_inc(s_ve2.h, 1)
                    s_ve2.v += 1
                TC = s_ve2.v

                if int(os.environ.get("KATT", "9")) < 4:
                    continue
                @blk.tensor
                def _(t, kc=kc, half=half, nkc=nkc, TC=TC, av_bank=av_bank):
                    t.wait_ge(s_ve2.h, TC)
                    mmi = t.matmul(av_bank[:, :],
                             vtok[:, 128 * kc:128 * (kc + 1)],
                             prT[:, half:half + 512],
                             start=(kc == 0), stop=(kc == nkc - 1))
                    if kc == nkc - 1:
                        mmi.then_inc(s_pe2.h, 1)
                        s_pe2.v += 1
            AV = s_pe2.v

            @blk.scalar
            def _(sc, h=h, qg=qg, AV=AV, av_bank=av_bank):
                sc.wait_ge(s_pe2.h, AV)
                sc.copy(outT[:, S * h + 512 * qg:S * h + 512 * (qg + 1)],
                        av_bank[:, :])
                sc.drain().then_inc(s_sc.h, 1)
                s_sc.v += 1
    s_sc.attn_exp_hist = exp_hist
    s_sc.attn_qti = qti[0]


def _wo(nc, blk, ap, b, outT, woB, wosb, cc_in, ps, sm):
    s_wkv = sm["s_wkv"]; s_wo = sm["s_wo"]; s_st = sm["s_st"]
    s_sc = sm["s_sc"]; s_woe = sm["s_woe"]; s_st2 = sm["s_st2"]
    st_sems = [s_st, s_st2]

    SC_END = s_sc.v

    @blk.gpsimd
    def _(g, b=b, WOE0=s_woe.v):
        g.wait_ge(s_sc.h, SC_END)
        if b == 1:
            g.wait_ge(s_wo.h, s_wo.v)
            g.wait_ge(s_woe.h, WOE0)
        for oc in range(4):
            g.dma_start(out=woB[:, DM * oc:DM * (oc + 1)],
                        in_=ap["woT"][128 * oc:128 * (oc + 1), :]
                        ).then_inc(s_wkv.h, 16)
            s_wkv.v += 16
    WOL = s_wkv.v

    evac_mark = getattr(s_woe, "wo_evac_marks", [0, 0])
    store_mark = getattr(s_st, "wo_store_marks", [0, 0])

    idx = 0
    for m in range(32):
        for tt in range(8):
            sl = idx % 2
            idx += 1
            bank = ps[sl]
            need_ev = evac_mark[sl]
            @blk.tensor
            def _(t, m=m, tt=tt, bank=bank, need_ev=need_ev,
                  first=(m == 0 and tt == 0)):
                t.wait_ge(s_wkv.h, WOL)
                if first:
                    t.wait_ge(s_sc.h, SC_END)
                if need_ev > 0:
                    t.wait_ge(s_woe.h, need_ev)
                for oc in range(4):
                    mmi = t.matmul(bank[:, :],
                             woB[:, DM * oc + 128 * m:DM * oc + 128 * (m + 1)],
                             outT[:, S * oc + 512 * tt:S * oc + 512 * (tt + 1)],
                             start=(oc == 0), stop=(oc == 3))
                mmi.then_inc(s_wo.h, 1)
                s_wo.v += 1
            WOP = s_wo.v

            need_st = store_mark[sl]
            @blk.scalar
            def _(sc, bank=bank, sl=sl, WOP=WOP, need_st=need_st):
                sc.wait_ge(s_wo.h, WOP)
                if need_st > 0:
                    sc.wait_ge(st_sems[sl].h, need_st)
                sc.copy(wosb[:, 512 * sl:512 * (sl + 1)], bank[:, :])
                sc.drain().then_inc(s_woe.h, 1)
                s_woe.v += 1
            evac_mark[sl] = s_woe.v
            WOE = s_woe.v

            @blk.sync
            def _(sy, m=m, tt=tt, b=b, sl=sl, WOE=WOE):
                sy.wait_ge(s_woe.h, WOE)
                gt = b * S + 512 * tt
                r = gt // TPC
                tl = gt % TPC
                sy.dma_start(out=cc_in[DM * r + 128 * m:DM * r + 128 * (m + 1),
                                       tl:tl + 512],
                             in_=wosb[:, 512 * sl:512 * (sl + 1)]
                             ).then_inc(st_sems[sl].h, 16)
                st_sems[sl].v += 16
            store_mark[sl] = st_sems[sl].v
    s_woe.wo_evac_marks = evac_mark
    s_st.wo_store_marks = store_mark


def kernel(**inputs):
    per_core = _host_prep(inputs)
    nc = build_program()
    res = run_bass_kernel_spmd(nc, per_core, core_ids=list(range(8)))
    outs = [res.results[c]["y"] for c in range(8)]
    full = np.concatenate(outs, axis=0)
    return full.reshape(B, S, DM)


# version 07
# speedup vs baseline: 1.0284x; 1.0284x over previous
import contextlib
import math

import numpy as np
import ml_dtypes

import concourse.bass as bass
import concourse.mybir as mybir
from concourse.alu_op_type import AluOpType as OP
from concourse.bass_utils import run_bass_kernel_spmd

F32 = mybir.dt.float32
BF16 = mybir.dt.bfloat16
U32 = mybir.dt.uint32
AF = mybir.ActivationFunctionType
AXX = mybir.AxisListType.X
BF = ml_dtypes.bfloat16

B = 2
S = 4096
DM = 4096
NH, NKV, HD = 32, 8, 128
T = B * S
TPC = T // 8
INTER = 11008
FRW = INTER // 8
FRP = 1408
IPAD = 8 * FRP
NFPC = IPAD // 128
NG = 32
NGD = 86
INV_SQRT_HD = 1.0 / math.sqrt(HD)
EPS = 1e-5
NEG = -1.0e9

import os
STOP_PHASE = int(os.environ.get("KSTOP", "99"))
KQKV = int(os.environ.get("KQKV", "9"))



def _host_prep(inputs):
    x = np.asarray(inputs["L0_attn_hidden_states"], dtype=np.float32)
    Wq = np.asarray(inputs["L0_attn_Wq"], dtype=np.float32)
    Wk = np.asarray(inputs["L0_attn_Wk"], dtype=np.float32)
    Wv = np.asarray(inputs["L0_attn_Wv"], dtype=np.float32)
    Wo = np.asarray(inputs["L0_attn_Wo"], dtype=np.float32)
    qwg = np.asarray(inputs["L0_ffn_qweight_gu"])
    qzg = np.asarray(inputs["L0_ffn_qzeros_gu"])
    sg = np.asarray(inputs["L0_ffn_scales_gu"], dtype=np.float32)
    qwd = np.asarray(inputs["L0_ffn_qweight_down"])
    qzd = np.asarray(inputs["L0_ffn_qzeros_down"])
    sd = np.asarray(inputs["L0_ffn_scales_down"], dtype=np.float32)
    g_gu = np.asarray(inputs["L0_ffn_g_idx_gu"]).astype(np.int64)
    g_dn = np.asarray(inputs["L0_ffn_g_idx_down"]).astype(np.int64)

    xf = x.reshape(T, DM)
    xT = np.ascontiguousarray(xf.T)

    n_idx = np.arange(IPAD) // FRP
    fr_idx = np.arange(IPAD) % FRP
    valid = fr_idx < FRW
    f_of_slot = np.where(valid, 8 * fr_idx + n_idx, 0)

    qwg_gate = np.ascontiguousarray(qwg[:, f_of_slot])
    qwg_up = np.ascontiguousarray(qwg[:, INTER + f_of_slot])
    s_slot = np.zeros((NG, 2 * IPAD), dtype=np.float32)
    s_slot[:, :IPAD] = np.where(valid[None, :], sg[:, f_of_slot], 0.0)
    s_slot[:, IPAD:] = np.where(valid[None, :], sg[:, INTER + f_of_slot], 0.0)

    pos = np.arange(S, dtype=np.float32)
    freqs = np.exp(-np.arange(0, HD, 2, dtype=np.float32)
                   * (math.log(10000.0) / HD))
    ang = pos[:, None] * freqs[None, :]
    cos_t = np.zeros((HD, S), dtype=np.float32)
    sin_t = np.zeros((HD, S), dtype=np.float32)
    cos_t[:64] = np.cos(ang).T
    cos_t[64:] = np.cos(ang).T
    sin_t[:64] = np.sin(ang).T
    sin_t[64:] = np.sin(ang).T

    ident = np.eye(128, dtype=BF)
    permrot = np.zeros((128, 128), dtype=np.float32)
    for m in range(64):
        permrot[m + 64, m] = -1.0
    for m in range(64, 128):
        permrot[m - 64, m] = 1.0
    permrot = permrot.astype(BF)

    masks = np.zeros((128, 4 * 512), dtype=np.float32)
    qq = np.arange(128)[:, None]
    kk = np.arange(512)[None, :]
    for r in range(4):
        masks[:, r * 512:(r + 1) * 512] = np.where(kk <= 128 * r + qq, 0.0, NEG)

    gones = np.zeros((128, 128), dtype=np.float32)
    for wc in range(4):
        for p in range(128):
            g = g_gu[8 * (128 * wc + p)]
            gones[p, 32 * wc + g] = 1.0
    gones = gones.astype(BF)

    dones = np.zeros((128, 11 * 88), dtype=np.float32)
    for c in range(11):
        for p in range(128):
            fr = 128 * c + p
            if fr < FRW:
                dones[p, c * 88 + g_dn[8 * fr]] = 1.0
    dones = dones.astype(BF)

    shared = {
        "xT": xT, "cos_t": cos_t, "sin_t": sin_t,
        "ident": ident, "permrot": permrot, "masks": masks,
        "gones": gones, "dones": dones, "s_slot": s_slot,
        "qwg_gate": qwg_gate, "qwg_up": qwg_up,
        "qzg": qzg, "qwd": np.ascontiguousarray(qwd), "qzd": qzd,
        "sdown": np.ascontiguousarray(sd),
    }

    WqT = np.ascontiguousarray(Wq.T)
    WkT = np.ascontiguousarray(Wk.T)
    WvT = np.ascontiguousarray(Wv.T)
    WoT = np.ascontiguousarray(Wo.T)

    per_core = []
    for c in range(8):
        wkvT = np.concatenate(
            [WqT[:, 512 * c:512 * (c + 1)],
             WkT[:, 128 * c:128 * (c + 1)],
             WvT[:, 128 * c:128 * (c + 1)]], axis=1)
        d = dict(shared)
        d["x_tok"] = np.ascontiguousarray(xf[TPC * c:TPC * (c + 1)])
        d["wkvT"] = np.ascontiguousarray(wkvT)
        d["woT"] = np.ascontiguousarray(WoT[512 * c:512 * (c + 1), :])
        per_core.append(d)
    return per_core


class Ctr:
    def __init__(self, h):
        self.h = h
        self.v = 0


def build_program():
    nc = bass.Bass("TRN2", target_bir_lowering=False)

    ap = {}
    def din(name, shape, dt):
        ap[name] = nc.dram_tensor(name, shape, dt, kind="ExternalInput").ap()
    din("xT", [DM, T], F32)
    din("x_tok", [TPC, DM], F32)
    din("wkvT", [DM, 768], F32)
    din("woT", [512, DM], F32)
    din("cos_t", [HD, S], F32)
    din("sin_t", [HD, S], F32)
    din("ident", [128, 128], BF16)
    din("permrot", [128, 128], BF16)
    din("masks", [128, 2048], F32)
    din("gones", [128, 128], BF16)
    din("dones", [128, 11 * 88], BF16)
    din("s_slot", [NG, 2 * IPAD], F32)
    din("qwg_gate", [512, IPAD], U32)
    din("qwg_up", [512, IPAD], U32)
    din("qzg", [NG, 2752], U32)
    din("qwd", [FRW, DM], U32)
    din("qzd", [NGD, 512], U32)
    din("sdown", [NGD, DM], F32)

    y = nc.dram_tensor("y", [TPC, DM], F32, kind="ExternalOutput").ap()
    KDBG = int(os.environ.get("KDBG", "0"))
    dbg = {"mask": KDBG}
    if KDBG:
        dbg["d_ccout"] = nc.dram_tensor("d_ccout", [DM, TPC], BF16,
                                        kind="ExternalOutput").ap()
        dbg["d_ccin"] = nc.dram_tensor("d_ccin", [DM, TPC], BF16,
                                       kind="ExternalOutput").ap()
        dbg["d_x1"] = nc.dram_tensor("d_x1", [TPC, DM], F32,
                                     kind="ExternalOutput").ap()
        dbg["d_h"] = nc.dram_tensor("d_h", [2, 128, TPC], BF16,
                                    kind="ExternalOutput").ap()
        dbg["d_x1tp"] = nc.dram_tensor("d_x1tp", [128, TPC], BF16,
                                       kind="ExternalOutput").ap()
        dbg["d_wg"] = nc.dram_tensor("d_wg", [128, 512], BF16,
                                     kind="ExternalOutput").ap()
        dbg["d_ag"] = nc.dram_tensor("d_ag", [128, 512], F32,
                                     kind="ExternalOutput").ap()
        dbg["d_xg"] = nc.dram_tensor("d_xg", [32, TPC], BF16,
                                     kind="ExternalOutput").ap()
        dbg["d_bg"] = nc.dram_tensor("d_bg", [32, 2 * FRP], BF16,
                                     kind="ExternalOutput").ap()
        dbg["d_wdd"] = nc.dram_tensor("d_wdd", [2, 128, DM], BF16,
                                      kind="ExternalOutput").ap()
        dbg["d_yf"] = nc.dram_tensor("d_yf", [2, 128, 512], BF16,
                                     kind="ExternalOutput").ap()
        dbg["d_bd"] = nc.dram_tensor("d_bd", [88, 512], BF16,
                                     kind="ExternalOutput").ap()
        dbg["d_zd"] = nc.dram_tensor("d_zd", [88, 512], U32,
                                     kind="ExternalOutput").ap()
        dbg["d_hg"] = nc.dram_tensor("d_hg", [88, 512], BF16,
                                     kind="ExternalOutput").ap()
        dbg["d_hfull"] = nc.dram_tensor("d_hfull", [NFPC, 128, TPC], BF16,
                                        kind="ExternalOutput").ap()

    cc_in = nc.dram_tensor("cc_in", [8 * DM, TPC], BF16).ap()
    cc_out = nc.dram_tensor("cc_out", [DM, TPC], BF16).ap()
    x1d = nc.dram_tensor("x1d", [TPC, DM], F32).ap()
    hdram = nc.dram_tensor("hdram", [NFPC, 128, TPC], BF16).ap()
    wdd = nc.dram_tensor("wdd", [NFPC, 128, DM], BF16).ap()

    with nc.Block() as blk:
        stack = contextlib.ExitStack()
        sems = {}
        names = ["s_const", "s_wkv", "s_x", "s_rope", "s_qkps", "s_ev",
                 "s_rp", "s_pe2", "s_sc", "s_ve2", "s_wo", "s_st", "s_cc",
                 "s_dd", "s_ddv", "s_wdd", "s_p4", "s_p4l", "s_p5l", "s_p5v",
                 "s_p5p", "s_p5s", "s_hst", "s_p6l", "s_p6p", "s_p6s",
                 "s_p6v", "s_out", "s_bar", "s_p4s", "s_p4t", "s_p6sq",
                 "s_woe", "s_msk", "s_st2", "s_wdd2", "s_dd2", "s_ddb",
                 "s_p6h", "s_p6sd", "s_p6x", "s_p6w0", "s_p6w1", "s_p6w2",
                 "s_x2", "s_x3", "s_x4"]
        for nm in names:
            sems[nm] = Ctr(stack.enter_context(nc.semaphore(nm)))
        with stack:
            _emit(nc, blk, ap, y, cc_in, cc_out, x1d, hdram, wdd, sems, dbg)
    return nc


def _barrier(blk, s_bar):
    @blk.tensor
    def _(t):
        t.drain().then_inc(s_bar.h, 1)
    @blk.vector
    def _(v):
        v.drain().then_inc(s_bar.h, 1)
    @blk.scalar
    def _(sc):
        sc.drain().then_inc(s_bar.h, 1)
    s_bar.v += 3
    tgt = s_bar.v
    @blk.tensor
    def _(t):
        t.wait_ge(s_bar.h, tgt)
    @blk.vector
    def _(v):
        v.wait_ge(s_bar.h, tgt)
    @blk.scalar
    def _(sc):
        sc.wait_ge(s_bar.h, tgt)
    @blk.sync
    def _(sy):
        sy.wait_ge(s_bar.h, tgt)
    @blk.gpsimd
    def _(g):
        g.wait_ge(s_bar.h, tgt)


def _emit(nc, blk, ap, y, cc_in, cc_out, x1d, hdram, wdd, sm, dbg=None):
    s_const = sm["s_const"]; s_wkv = sm["s_wkv"]; s_x = sm["s_x"]
    s_rope = sm["s_rope"]; s_qkps = sm["s_qkps"]; s_ev = sm["s_ev"]
    s_rp = sm["s_rp"]; s_pe2 = sm["s_pe2"]; s_sc = sm["s_sc"]
    s_ve2 = sm["s_ve2"]; s_wo = sm["s_wo"]; s_st = sm["s_st"]
    s_cc = sm["s_cc"]; s_dd = sm["s_dd"]; s_ddv = sm["s_ddv"]
    s_wdd = sm["s_wdd"]; s_p4 = sm["s_p4"]; s_p4l = sm["s_p4l"]
    s_p5l = sm["s_p5l"]; s_p5v = sm["s_p5v"]; s_p5p = sm["s_p5p"]
    s_p5s = sm["s_p5s"]; s_hst = sm["s_hst"]; s_p6l = sm["s_p6l"]
    s_p6p = sm["s_p6p"]; s_p6s = sm["s_p6s"]; s_p6v = sm["s_p6v"]
    s_out = sm["s_out"]; s_bar = sm["s_bar"]
    s_p4s = sm["s_p4s"]; s_p4t = sm["s_p4t"]; s_p6sq = sm["s_p6sq"]

    st2 = contextlib.ExitStack()
    with st2:
        ec = st2.enter_context
        ident = ec(nc.sbuf_tensor("identb", [128, 128], BF16))
        prot = ec(nc.sbuf_tensor("protb", [128, 128], BF16))
        gonesb = ec(nc.sbuf_tensor("gonesb", [128, 128], BF16))
        donesb = ec(nc.sbuf_tensor("donesb", [128, 11 * 88], BF16))
        stats = ec(nc.sbuf_tensor("stats", [128, 64], F32))
        mv = ec(nc.sbuf_tensor("mvt", [128, 2], F32))
        veps = ec(nc.sbuf_tensor("veps", [128, 1], F32))
        stdv = ec(nc.sbuf_tensor("stdv", [128, 1], F32))
        rstd = ec(nc.sbuf_tensor("rstd", [128, 1], F32))
        rmax = ec(nc.sbuf_tensor("rmax", [128, 4], F32))
        nmax = ec(nc.sbuf_tensor("nmax", [128, 4], F32))
        ssum = ec(nc.sbuf_tensor("ssum", [128, 4], F32))
        rsum = ec(nc.sbuf_tensor("rsum", [128, 4], F32))

        @blk.sync
        def _(sy):
            for nm, t in [("ident", ident), ("permrot", prot),
                          ("gones", gonesb), ("dones", donesb)]:
                sy.dma_start(out=t[:, :], in_=ap[nm][:, :]).then_inc(s_const.h, 16)
                s_const.v += 16
        CONST = s_const.v

        st3 = contextlib.ExitStack()
        with st3:
            ec3 = st3.enter_context
            arena = ec3(nc.sbuf_tensor("arena", [128, 40960], BF16))
            scrx = ec3(nc.sbuf_tensor("scrx", [128, 4096], F32))
            maskb = ec3(nc.sbuf_tensor("maskb", [128, 2048], F32))
            q_sb = ec3(nc.sbuf_tensor("q_sb", [128, 4 * S], BF16))
            k_sb = ec3(nc.sbuf_tensor("k_sb", [128, S], BF16))
            vtok = ec3(nc.sbuf_tensor("vtok", [128, S], BF16))
            prT = ec3(nc.sbuf_tensor("prT", [128, 2 * 512], BF16))
            ropes = ec3(nc.sbuf_tensor("ropes", [128, 1024], F32))
            qraw = ec3(nc.sbuf_tensor("qraw", [128, 6 * 512], BF16))
            rt1 = ec3(nc.sbuf_tensor("rt1", [128, 512], F32))
            rt2 = ec3(nc.sbuf_tensor("rt2", [128, 512], F32))
            wosb = ec3(nc.sbuf_tensor("wosb", [128, 2 * 512], BF16))

            ps = [ec3(nc.psum_tensor(f"ps{i}", [128, 512], F32))
                  for i in range(6)]
            psb = ec3(nc.psum_tensor("psb", [128, 512], BF16))
            psb2 = ec3(nc.psum_tensor("psb2", [128, 512], BF16))
            psbp = [psb, psb2]

            @blk.sync
            def _(sy):
                sy.dma_start(out=maskb[:, :], in_=ap["masks"][:, :]
                             ).then_inc(sm["s_msk"].h, 16)
                sm["s_msk"].v += 16
            sm["s_msk"].mask_mark = sm["s_msk"].v

            wkv = arena[:, 0:24576]
            xst = arena[:, 24576:40960]
            scr0 = scrx[:, :]
            scr1 = arena[:, 0:8192].bitcast(F32)
            prb = arena[:, 8192:24576]
            outT = arena[:, 24576:40960]
            woB = arena[:, 0:16384]

            nbatch = 0 if STOP_PHASE < 1 else (1 if STOP_PHASE < 4 else B)
            for b in range(nbatch):
                _qkv(nc, blk, ap, b, wkv, xst, ropes, q_sb, k_sb, vtok,
                     qraw, rt1, rt2, ps, psb, ident, prot, sm, CONST)
                if STOP_PHASE >= 2:
                    _attn(nc, blk, b, q_sb, k_sb, vtok, outT, [scr0, scr1],
                          prb, prT, maskb, rmax, nmax, ssum, rsum, ps, psbp,
                          ident, sm)
                if STOP_PHASE >= 3:
                    _wo(nc, blk, ap, b, outT, woB, wosb, cc_in, ps, sm)

        _barrier(blk, s_bar)

        if STOP_PHASE < 5:
            _dummy_out(nc, blk, y, s_out)
            return

        @blk.gpsimd
        def _(g):
            g.wait_ge(s_st.h, s_st.v)
            g.wait_ge(sm["s_st2"].h, sm["s_st2"].v)
            g.collective_compute(
                "ReduceScatter", OP.add,
                replica_groups=[list(range(8))],
                ins=[cc_in.opt()],
                outs=[cc_out.opt()],
            ).then_inc(s_cc.h, 1)
            s_cc.v += 1

        if dbg and (dbg["mask"] & 1):
            @blk.sync
            def _(sy):
                sy.wait_ge(sm["s_st"].h, sm["s_st"].v)
                sy.dma_start(out=dbg["d_ccin"][:, :], in_=cc_in[0:DM, :]
                             ).then_inc(sm["s_out"].h, 16)
                sm["s_out"].v += 16
                sy.wait_ge(sm["s_cc"].h, 1)
                sy.dma_start(out=dbg["d_ccout"][:, :], in_=cc_out[:, :]
                             ).then_inc(sm["s_out"].h, 16)
                sm["s_out"].v += 16

        st4 = contextlib.ExitStack()
        with st4:
            ec4 = st4.enter_context
            qwdb = [ec4(nc.sbuf_tensor(f"qwdb{i}", [128, DM], U32))
                    for i in range(2)]
            add_t = ec4(nc.sbuf_tensor("add_t", [128, DM], F32))
            nibd = ec4(nc.sbuf_tensor("nibd", [128, DM], U32))
            wdt = ec4(nc.sbuf_tensor("wdt", [128, 2 * DM], BF16))
            wdt_store = [0, 0]

            @blk.vector
            def _(v):
                v.memset(add_t[96:128, :], 0.0)
                v.drain().then_inc(s_ddv.h, 1)
                s_ddv.v += 1

            ST_ALL = s_st.v
            s_dd2 = sm["s_dd2"]; s_ddb = sm["s_ddb"]; s_wdd2 = sm["s_wdd2"]
            dd_sems = [s_dd, s_dd2]
            wdd_sems = [s_wdd, s_wdd2]
            for c in range(11):
                rows = 128 if c < 10 else FRW - 1280
                buf = qwdb[c % 2]
                bsem = dd_sems[c % 2]
                @blk.sync
                def _(sy, c=c, rows=rows, buf=buf, bsem=bsem):
                    if c == 0:
                        sy.wait_ge(s_st.h, ST_ALL)
                        sy.wait_ge(sm["s_st2"].h, sm["s_st2"].v)
                    if c >= 2:
                        sy.wait_ge(s_ddv.h, 1 + 8 * (c - 1))
                    sy.dma_start(out=buf[0:rows, :],
                                 in_=ap["qwd"][128 * c:128 * c + rows, :]
                                 ).then_inc(bsem.h, 16)
                    bsem.v += 16

                @blk.gpsimd
                def _(g, c=c):
                    g.wait_ge(s_ddv.h, 1 + (8 * c if c > 0 else 0))
                    glo = 8 * c
                    gn = min(NGD - glo, 8)
                    g.dma_start(
                        out=add_t[0:16 * gn, :],
                        in_=ap["sdown"][glo:glo + gn, :].unsqueeze(1)
                        .broadcast_to([gn, 16, DM])).then_inc(s_ddb.h, 16)
                    s_ddb.v += 16
                LD = bsem.v
                LB = s_ddb.v

                for n in range(8):
                    sl = (8 * c + n) % 2
                    @blk.vector
                    def _(v, c=c, n=n, rows=rows, buf=buf, LD=LD, LB=LB,
                          sl=sl, bsem=bsem, need_st=wdt_store[sl]):
                        v.wait_ge(bsem.h, LD)
                        v.wait_ge(s_ddb.h, LB)
                        if need_st > 0:
                            v.wait_ge(wdd_sems[sl].h, need_st)
                        v.tensor_scalar(nibd[:, :], buf[:, :],
                                        28 - 4 * n, 28,
                                        OP.logical_shift_left,
                                        OP.logical_shift_right)
                        v.drain()
                        v.tensor_tensor(wdt[:, DM * sl:DM * sl + DM],
                                        nibd[:, :],
                                        add_t[:, :], OP.mult)
                        v.drain().then_inc(s_ddv.h, 1)
                        s_ddv.v += 1

                    @blk.sync
                    def _(sy, c=c, n=n, sl=sl):
                        sy.wait_ge(s_ddv.h, s_ddv.v)
                        sy.dma_start(out=wdd[n * 11 + c, :, :],
                                     in_=wdt[:, DM * sl:DM * sl + DM]
                                     ).then_inc(wdd_sems[sl].h, 16)
                        wdd_sems[sl].v += 16
                    wdt_store[sl] = wdd_sems[sl].v

        if STOP_PHASE < 6:
            _barrier(blk, s_bar)
            _dummy_out(nc, blk, y, s_out, wait=[(sm["s_cc"].h, 1),
                                               (sm["s_wdd"].h, sm["s_wdd"].v)])
            return

        stP45 = contextlib.ExitStack()
        stP45.__enter__()
        x1tp = stP45.enter_context(nc.sbuf_tensor("x1tp", [128, 32 * TPC], BF16))
        xgT = stP45.enter_context(nc.sbuf_tensor("xgT", [32, TPC], BF16))
        xgL = stP45.enter_context(nc.sbuf_tensor("xgL", [32, TPC], BF16))
        xgF = stP45.enter_context(nc.sbuf_tensor("xgF", [32, TPC], F32))
        st5 = contextlib.ExitStack()
        with st5:
            ec5 = st5.enter_context
            at_tok = ec5(nc.sbuf_tensor("at_tok", [128, DM], BF16))
            xr = ec5(nc.sbuf_tensor("xr", [128, DM], F32))
            x1f = ec5(nc.sbuf_tensor("x1f", [128, DM], F32))
            x1b = ec5(nc.sbuf_tensor("x1b", [128, DM], BF16))
            x1bp = ec5(nc.sbuf_tensor("x1bp", [128, DM], BF16))
            psg = ec5(nc.psum_tensor("psg", [32, 512], F32))
            psb4 = [ec5(nc.psum_tensor(f"psb4_{i}", [128, 512], BF16))
                    for i in range(2)]

            for tt in range(8):
                @blk.sync
                def _(sy, tt=tt):
                    if tt == 0:
                        sy.wait_ge(s_cc.h, 1)
                        sy.wait_ge(s_wdd.h, s_wdd.v)
                        sy.wait_ge(sm["s_wdd2"].h, sm["s_wdd2"].v)
                        sy.wait_ge(s_dd.h, s_dd.v)
                        sy.wait_ge(sm["s_dd2"].h, sm["s_dd2"].v)
                        sy.wait_ge(sm["s_ddb"].h, sm["s_ddb"].v)
                    else:
                        sy.wait_ge(s_p4.h, s_p4.v)
                    sy.dma_start(out=at_tok[:, :],
                                 in_=cc_out[:, 128 * tt:128 * (tt + 1)],
                                 transpose=True).then_inc(s_p4l.h, 16)
                    s_p4l.v += 16
                    sy.dma_start(out=xr[:, :],
                                 in_=ap["x_tok"][128 * tt:128 * (tt + 1), :]
                                 ).then_inc(s_p4l.h, 16)
                    s_p4l.v += 16

                @blk.vector
                def _(v, tt=tt):
                    v.wait_ge(s_p4l.h, s_p4l.v)
                    v.tensor_tensor(xr[:, :], xr[:, :], at_tok[:, :], OP.add)
                    v.drain()
                    for j in range(8):
                        v.bn_stats(stats[:, 6 * j:6 * (j + 1)],
                                   xr[:, 512 * j:512 * (j + 1)])
                    v.drain()
                    v.bn_aggr(mv[:, :], stats[:, 0:48])
                    v.drain()
                    v.tensor_scalar(veps[:, :], mv[:, 1:2], EPS, None, OP.add)
                    v.drain().then_inc(s_p4.h, 1)
                    s_p4.v += 1

                @blk.scalar
                def _(sc):
                    sc.wait_ge(s_p4.h, s_p4.v)
                    sc.activation(stdv[:, :], veps[:, :], AF.Sqrt).then_inc(s_p4s.h, 1)
                    s_p4s.v += 1

                @blk.vector
                def _(v, tt=tt, SQ=s_p4s.v, STM=s_st.v):
                    v.wait_ge(s_p4s.h, SQ)
                    if tt > 0:
                        v.wait_ge(s_st.h, STM)
                    v.reciprocal(rstd[:, :], stdv[:, :])
                    v.drain()
                    v.tensor_scalar(x1f[:, :], xr[:, :], mv[:, 0:1], rstd[:, :],
                                    OP.subtract, OP.mult)
                    v.drain()
                    v.tensor_copy(x1b[:, :], x1f[:, :])
                    v.drain()
                    v.tensor_copy(
                        x1bp[:, :].rearrange("p (nn wr) -> p nn wr", nn=8),
                        x1b[:, :].rearrange("p (wr nn) -> p nn wr", nn=8))
                    v.drain().then_inc(s_p4.h, 1)
                    s_p4.v += 1

                @blk.sync
                def _(sy, tt=tt):
                    sy.wait_ge(s_p4.h, s_p4.v)
                    sy.dma_start(out=x1d[128 * tt:128 * (tt + 1), :],
                                 in_=x1f[:, :]).then_inc(s_st.h, 16)
                    s_st.v += 16

                X1B = s_p4.v
                if int(os.environ.get("KP4", "9")) < 2:
                    continue
                for hc in range(32):
                    n, wc = hc // 4, hc % 4
                    @blk.tensor
                    def _(t, tt=tt, hc=hc, n=n, wc=wc, X1B=X1B, CP=s_p4.v):
                        if tt == 0 and hc == 0:
                            t.wait_ge(s_const.h, CONST)
                        t.wait_ge(s_p4.h, X1B)
                        if hc >= 2 or tt > 0:
                            t.wait_ge(s_p4.h, CP - 1)
                        t.transpose(psb4[hc % 2][:, 0:128],
                                    x1bp[:, 128 * hc:128 * (hc + 1)],
                                    ident[:, :]).then_inc(s_p4t.h, 1)
                        s_p4t.v += 1

                    if int(os.environ.get("KP4", "9")) < 3:
                        @blk.vector
                        def _(v):
                            v.engine_nop().then_inc(s_p4.h, 1)
                            s_p4.v += 1
                        continue
                    @blk.vector
                    def _(v, tt=tt, hc=hc, TR=s_p4t.v):
                        v.wait_ge(s_p4t.h, TR)
                        v.tensor_copy(
                            x1tp[:, TPC * hc + 128 * tt:TPC * hc + 128 * (tt + 1)],
                            psb4[hc % 2][:, 0:128]).then_inc(s_p4.h, 1)
                        s_p4.v += 1

            if int(os.environ.get("KP4", "9")) < 4:
                _barrier(blk, s_bar)
                _dummy_out(nc, blk, y, s_out)
                stP45.__exit__(None, None, None)
                return
            for th in range(2):
                @blk.tensor
                def _(t, th=th, CP=s_p4.v):
                    t.wait_ge(s_p4.h, CP)
                    if th == 1:
                        t.wait_ge(s_p4.h, s_p4.v)
                    for hc in range(32):
                        wc = hc % 4
                        mmi = t.matmul(psg[:, :],
                                 gonesb[:, 32 * wc:32 * (wc + 1)],
                                 x1tp[:, TPC * hc + 512 * th:TPC * hc + 512 * (th + 1)],
                                 start=(hc == 0), stop=(hc == 31))
                    mmi.then_inc(s_p4t.h, 1)
                    s_p4t.v += 1

                @blk.vector
                def _(v, th=th, MG=s_p4t.v):
                    v.wait_ge(s_p4t.h, MG)
                    sl5 = slice(512 * th, 512 * (th + 1))
                    v.tensor_scalar(xgF[:, sl5], psg[:, :],
                                    -1.0, None, OP.mult)
                    v.drain()
                    v.tensor_copy(xgT[:, sl5], xgF[:, sl5])
                    v.drain()
                    v.tensor_tensor(xgF[:, sl5], xgF[:, sl5], xgT[:, sl5],
                                    OP.subtract)
                    v.drain()
                    v.tensor_copy(xgL[:, sl5], xgF[:, sl5])
                    v.drain().then_inc(s_p4.h, 1)
                    s_p4.v += 1

        _barrier(blk, s_bar)

        if STOP_PHASE < 7:
            _dummy_out(nc, blk, y, s_out)
            stP45.__exit__(None, None, None)
            return

        st6 = contextlib.ExitStack()
        with st6:
            ec6 = st6.enter_context
            qg_sb = ec6(nc.sbuf_tensor("qg_sb", [128, 4 * 512], U32))
            qu_sb = ec6(nc.sbuf_tensor("qu_sb", [128, 4 * 512], U32))
            ag_sb = ec6(nc.sbuf_tensor("ag_sb", [128, 4 * 512], F32))
            au_sb = ec6(nc.sbuf_tensor("au_sb", [128, 4 * 512], F32))
            wg_sb = ec6(nc.sbuf_tensor("wg_sb", [128, 32 * 512], BF16))
            wu_sb = ec6(nc.sbuf_tensor("wu_sb", [128, 32 * 512], BF16))
            nib5 = ec6(nc.sbuf_tensor("nib5", [128, 1024], U32))
            sgate = ec6(nc.sbuf_tensor("sgate", [128, 512], BF16))
            h_sb = ec6(nc.sbuf_tensor("h_sb", [128, 2 * 512], BF16))
            zrow = ec6(nc.sbuf_tensor("zrow", [32, 512], U32))
            srow = ec6(nc.sbuf_tensor("srow", [32, 1024], F32))
            bg_sb2 = ec6(nc.sbuf_tensor("bg_sb2", [32, 4 * FRP], BF16))
            qz_sb = ec6(nc.sbuf_tensor("qz_sb", [32, 1024], U32))

            ps5 = [ec6(nc.psum_tensor(f"ps5_{i}", [128, 512], F32))
                   for i in range(4)]

            @blk.sync
            def _(sy):
                sy.wait_ge(s_wdd.h, s_wdd.v)
                sy.wait_ge(sm["s_wdd2"].h, sm["s_wdd2"].v)
                sy.wait_ge(s_dd.h, s_dd.v)
                sy.wait_ge(sm["s_dd2"].h, sm["s_dd2"].v)
                sy.wait_ge(s_st.h, s_st.v)

            @blk.gpsimd
            def _(g):
                g.wait_ge(sm["s_ddb"].h, sm["s_ddb"].v)
                g.wait_ge(s_st.h, s_st.v)
                g.wait_ge(s_p4l.h, s_p4l.v)

            franges = [(0, 512), (512, 512), (1024, 384)]
            first_group = True
            pair_done = [None, None]
            sg_used = None
            gidx = 0
            DQB_prev = 0
            for (fr0, frw) in franges:
                vw = min(FRW - fr0, frw)
                for n in range(8):
                    slot0 = n * FRP + fr0
                    bg_sb = bg_sb2[:, (2 * FRP) * (gidx % 2):(2 * FRP) * (gidx % 2) + 2 * FRP]
                    gidx += 1
                    MM_PREV = s_p5p.v
                    @blk.sync
                    def _(sy, fr0=fr0, frw=frw, n=n, slot0=slot0,
                          first=first_group, DQB_prev=DQB_prev,
                          vw_0=vw):
                        if not first:
                            sy.wait_ge(s_p5v.h, DQB_prev)
                        for wc in range(4):
                            sy.dma_start(
                                out=qg_sb[:, 512 * wc:512 * wc + frw],
                                in_=ap["qwg_gate"][128 * wc:128 * (wc + 1),
                                                   slot0:slot0 + frw]
                            ).then_inc(s_p5l.h, 16)
                            s_p5l.v += 16
                            sy.dma_start(
                                out=qu_sb[:, 512 * wc:512 * wc + frw],
                                in_=ap["qwg_up"][128 * wc:128 * (wc + 1),
                                                 slot0:slot0 + frw]
                            ).then_inc(s_p5l.h, 16)
                            s_p5l.v += 16
                        sy.dma_start(out=srow[:, 0:vw_0],
                                     in_=ap["s_slot"][:, slot0:slot0 + vw_0]
                                     ).then_inc(s_p5l.h, 16)
                        s_p5l.v += 16
                        sy.dma_start(out=srow[:, 512:512 + vw_0],
                                     in_=ap["s_slot"][:, IPAD + slot0:IPAD + slot0 + vw_0]
                                     ).then_inc(s_p5l.h, 16)
                        s_p5l.v += 16
                        vw_ = vw_0
                        sy.dma_start(out=qz_sb[:, 0:vw_],
                                     in_=ap["qzg"][:, fr0:fr0 + vw_]
                                     ).then_inc(s_p5l.h, 16)
                        s_p5l.v += 16
                        sy.dma_start(out=qz_sb[:, 512:512 + vw_],
                                     in_=ap["qzg"][:, FRW + fr0:FRW + fr0 + vw_]
                                     ).then_inc(s_p5l.h, 16)
                        s_p5l.v += 16

                    s_ddb = sm["s_ddb"]
                    @blk.gpsimd
                    def _(g, fr0=fr0, frw=frw, n=n, first=first_group,
                          DQB_prev=DQB_prev):
                        if not first:
                            g.wait_ge(s_p5v.h, DQB_prev)
                        for wc in range(4):
                            g.dma_start(
                                out=ag_sb[:, 512 * wc:512 * wc + frw],
                                in_=ap["s_slot"][8 * wc:8 * (wc + 1),
                                                 n * FRP + fr0:n * FRP + fr0 + frw]
                                .unsqueeze(1).broadcast_to([8, 16, frw])
                            ).then_inc(s_ddb.h, 16)
                            s_ddb.v += 16
                            g.dma_start(
                                out=au_sb[:, 512 * wc:512 * wc + frw],
                                in_=ap["s_slot"][8 * wc:8 * (wc + 1),
                                                 IPAD + n * FRP + fr0:IPAD + n * FRP + fr0 + frw]
                                .unsqueeze(1).broadcast_to([8, 16, frw])
                            ).then_inc(s_ddb.h, 16)
                            s_ddb.v += 16
                    LD = s_p5l.v
                    LDB = s_ddb.v

                    @blk.vector
                    def _(v, fr0=fr0, frw=frw, vw=vw, n=n, LD=LD, LDB=LDB,
                          MM_PREV=MM_PREV, bg_sb=bg_sb):
                        v.wait_ge(s_p5l.h, LD)
                        v.wait_ge(sm["s_ddb"].h, LDB)
                        if MM_PREV > 0:
                            v.wait_ge(s_p5p.h, MM_PREV - 1)
                        if vw < frw:
                            v.memset(bg_sb[:, 0:frw], 0.0)
                            v.memset(bg_sb[:, FRP:FRP + frw], 0.0)
                            v.drain()
                        v.tensor_scalar(zrow[:, 0:vw], qz_sb[:, 0:vw],
                                        28 - 4 * n, 28,
                                        OP.logical_shift_left,
                                        OP.logical_shift_right)
                        v.drain()
                        v.scalar_tensor_tensor(bg_sb[:, 0:vw], zrow[:, 0:vw],
                                               1.0, srow[:, 0:vw],
                                               OP.add, OP.mult)
                        v.drain()
                        v.tensor_scalar(zrow[:, 0:vw], qz_sb[:, 512:512 + vw],
                                        28 - 4 * n, 28,
                                        OP.logical_shift_left,
                                        OP.logical_shift_right)
                        v.drain()
                        v.scalar_tensor_tensor(bg_sb[:, FRP:FRP + vw],
                                               zrow[:, 0:vw],
                                               1.0, srow[:, 512:512 + vw],
                                               OP.add, OP.mult)
                        v.drain()
                        for hc in range(16):
                            nh, wc = hc // 4, hc % 4
                            v.tensor_scalar(nib5[:, 0:frw],
                                            qg_sb[:, 512 * wc:512 * wc + frw],
                                            28 - 4 * nh, 28,
                                            OP.logical_shift_left,
                                            OP.logical_shift_right)
                            v.tensor_scalar(nib5[:, 512:512 + frw],
                                            qu_sb[:, 512 * wc:512 * wc + frw],
                                            28 - 4 * nh, 28,
                                            OP.logical_shift_left,
                                            OP.logical_shift_right)
                            v.drain()
                            v.tensor_tensor(wg_sb[:, 512 * hc:512 * hc + frw],
                                            nib5[:, 0:frw],
                                            ag_sb[:, 512 * wc:512 * wc + frw],
                                            OP.mult)
                            v.tensor_tensor(wu_sb[:, 512 * hc:512 * hc + frw],
                                            nib5[:, 512:512 + frw],
                                            au_sb[:, 512 * wc:512 * wc + frw],
                                            OP.mult)
                            v.drain()
                        v.drain().then_inc(s_p5v.h, 1)
                        s_p5v.v += 1
                    DQA = s_p5v.v

                    @blk.vector
                    def _(v, frw=frw, MM_PREV=MM_PREV):
                        if MM_PREV > 0:
                            v.wait_ge(s_p5p.h, MM_PREV)
                        for hc in range(16, 32):
                            nh, wc = hc // 4, hc % 4
                            v.tensor_scalar(nib5[:, 0:frw],
                                            qg_sb[:, 512 * wc:512 * wc + frw],
                                            28 - 4 * nh, 28,
                                            OP.logical_shift_left,
                                            OP.logical_shift_right)
                            v.tensor_scalar(nib5[:, 512:512 + frw],
                                            qu_sb[:, 512 * wc:512 * wc + frw],
                                            28 - 4 * nh, 28,
                                            OP.logical_shift_left,
                                            OP.logical_shift_right)
                            v.drain()
                            v.tensor_tensor(wg_sb[:, 512 * hc:512 * hc + frw],
                                            nib5[:, 0:frw],
                                            ag_sb[:, 512 * wc:512 * wc + frw],
                                            OP.mult)
                            v.tensor_tensor(wu_sb[:, 512 * hc:512 * hc + frw],
                                            nib5[:, 512:512 + frw],
                                            au_sb[:, 512 * wc:512 * wc + frw],
                                            OP.mult)
                            v.drain()
                        v.drain().then_inc(s_p5v.h, 1)
                        s_p5v.v += 1
                    DQ = s_p5v.v
                    DQB_prev = DQ
                    if dbg and (dbg["mask"] & 2) and fr0 == 0 and n == 0:
                        @blk.sync
                        def _(sy, DQ=DQ):
                            sy.wait_ge(s_p5v.h, DQ)
                            for nm, src_ap in [
                                    ("d_x1tp", x1tp[:, 0:TPC]),
                                    ("d_wg", wg_sb[:, 0:512]),
                                    ("d_ag", ag_sb[:, 0:512]),
                                    ("d_xg", xgT[:, :]),
                                    ("d_bg", bg_sb[:, :])]:
                                sy.dma_start(out=dbg[nm][:, :] if dbg[nm].ndim == 2 else dbg[nm],
                                             in_=src_ap).then_inc(s_out.h, 16)
                                s_out.v += 16

                    nchunk = (frw + 127) // 128
                    for ci in range(nchunk):
                        fpc = (n * FRP + fr0) // 128 + ci
                        for th in range(2):
                            gbank = ps5[0 if th == 0 else 2]
                            ubank = ps5[1 if th == 0 else 3]
                            EV_PREV = pair_done[th]
                            @blk.tensor
                            def _(t, ci=ci, th=th, gbank=gbank, ubank=ubank,
                                  DQA=DQA, EV_PREV=EV_PREV):
                                t.wait_ge(s_p5v.h, DQA)
                                if EV_PREV is not None:
                                    t.wait_ge(s_p5s.h, EV_PREV[0])
                                    t.wait_ge(s_p5v.h, EV_PREV[1])
                                for hc in range(16):
                                    t.matmul(
                                        gbank[:, :],
                                        wg_sb[:, 512 * hc + 128 * ci:512 * hc + 128 * (ci + 1)],
                                        x1tp[:, TPC * hc + 512 * th:TPC * hc + 512 * (th + 1)],
                                        start=(hc == 0), stop=False)
                                    mmi = t.matmul(
                                        ubank[:, :],
                                        wu_sb[:, 512 * hc + 128 * ci:512 * hc + 128 * (ci + 1)],
                                        x1tp[:, TPC * hc + 512 * th:TPC * hc + 512 * (th + 1)],
                                        start=(hc == 0), stop=False)
                                mmi.then_inc(s_p5p.h, 1)
                                s_p5p.v += 1

                            @blk.tensor
                            def _(t, ci=ci, th=th, gbank=gbank, ubank=ubank,
                                  DQ=DQ, bg_sb=bg_sb):
                                t.wait_ge(s_p5v.h, DQ)
                                for hc in range(16, 32):
                                    t.matmul(
                                        gbank[:, :],
                                        wg_sb[:, 512 * hc + 128 * ci:512 * hc + 128 * (ci + 1)],
                                        x1tp[:, TPC * hc + 512 * th:TPC * hc + 512 * (th + 1)],
                                        start=False, stop=False)
                                    t.matmul(
                                        ubank[:, :],
                                        wu_sb[:, 512 * hc + 128 * ci:512 * hc + 128 * (ci + 1)],
                                        x1tp[:, TPC * hc + 512 * th:TPC * hc + 512 * (th + 1)],
                                        start=False, stop=False)
                                t.matmul(gbank[:, :],
                                         bg_sb[:, 128 * ci:128 * (ci + 1)],
                                         xgT[:, 512 * th:512 * (th + 1)],
                                         start=False, stop=False)
                                t.matmul(gbank[:, :],
                                         bg_sb[:, 128 * ci:128 * (ci + 1)],
                                         xgL[:, 512 * th:512 * (th + 1)],
                                         start=False, stop=True)
                                t.matmul(ubank[:, :],
                                         bg_sb[:, FRP + 128 * ci:FRP + 128 * (ci + 1)],
                                         xgT[:, 512 * th:512 * (th + 1)],
                                         start=False, stop=False)
                                t.matmul(ubank[:, :],
                                         bg_sb[:, FRP + 128 * ci:FRP + 128 * (ci + 1)],
                                         xgL[:, 512 * th:512 * (th + 1)],
                                         start=False, stop=True).then_inc(s_p5p.h, 1)
                                s_p5p.v += 1
                            MM = s_p5p.v

                            @blk.scalar
                            def _(sc, gbank=gbank, MM=MM, prev=sg_used):
                                sc.wait_ge(s_p5p.h, MM)
                                if prev is not None:
                                    sc.wait_ge(s_p5v.h, prev)
                                sc.activation(sgate[:, :], gbank[:, :],
                                              AF.Silu).then_inc(s_p5s.h, 1)
                                s_p5s.v += 1
                            SG = s_p5s.v

                            @blk.vector
                            def _(v, th=th, ubank=ubank, SG=SG):
                                v.wait_ge(s_p5s.h, SG)
                                if s_hst.v > 0:
                                    v.wait_ge(s_hst.h, s_hst.v)
                                v.tensor_tensor(h_sb[:, 512 * th:512 * (th + 1)],
                                                sgate[:, :], ubank[:, :],
                                                OP.mult)
                                v.drain().then_inc(s_p5v.h, 1)
                                s_p5v.v += 1
                            pair_done[th] = (SG, s_p5v.v)
                            sg_used = s_p5v.v

                            @blk.sync
                            def _(sy, fpc=fpc, th=th, HT=s_p5v.v):
                                sy.wait_ge(s_p5v.h, HT)
                                sy.dma_start(
                                    out=hdram[fpc, :, 512 * th:512 * (th + 1)],
                                    in_=h_sb[:, 512 * th:512 * (th + 1)]
                                ).then_inc(s_hst.h, 16)
                                s_hst.v += 16
                    first_group = False

        stP45.__exit__(None, None, None)
        _barrier(blk, s_bar)

        if dbg and (dbg["mask"] & 4):
            @blk.sync
            def _(sy):
                sy.wait_ge(s_p4.h, s_p4.v)
                sy.wait_ge(s_st.h, s_st.v)
                sy.dma_start(out=dbg["d_x1"][:, :], in_=x1d[:, :]
                             ).then_inc(s_out.h, 16)
                s_out.v += 16
                sy.wait_ge(s_hst.h, s_hst.v)
                sy.dma_start(out=dbg["d_h"][0, :, :], in_=hdram[0, :, :]
                             ).then_inc(s_out.h, 16)
                s_out.v += 16
                sy.dma_start(out=dbg["d_h"][1, :, :], in_=hdram[87, :, :]
                             ).then_inc(s_out.h, 16)
                s_out.v += 16
                sy.dma_start(out=dbg["d_hfull"][:, :, :], in_=hdram[:, :, :]
                             ).then_inc(s_out.h, 16)
                s_out.v += 16
                sy.wait_ge(sm["s_wdd"].h, sm["s_wdd"].v)
                sy.dma_start(out=dbg["d_wdd"][0, :, :], in_=wdd[0, :, :]
                             ).then_inc(s_out.h, 16)
                s_out.v += 16
                sy.dma_start(out=dbg["d_wdd"][1, :, :], in_=wdd[87, :, :]
                             ).then_inc(s_out.h, 16)
                s_out.v += 16

        if STOP_PHASE < 8:
            _dummy_out(nc, blk, y, s_out)
            return

        st7 = contextlib.ExitStack()
        with st7:
            ec7 = st7.enter_context
            hh = ec7(nc.sbuf_tensor("hh", [128, NFPC * 512], BF16))
            wdt6 = [ec7(nc.sbuf_tensor(f"wdt6_{i}", [128, 512], BF16))
                    for i in range(3)]
            bd_sb = ec7(nc.sbuf_tensor("bd_sb", [88, 512], BF16))
            zd_sb = ec7(nc.sbuf_tensor("zd_sb", [88, 512], U32))
            sd_sb = ec7(nc.sbuf_tensor("sd_sb", [88, 512], F32))
            qzd_sb = ec7(nc.sbuf_tensor("qzd_sb", [88, 512], U32))
            hgT = ec7(nc.sbuf_tensor("hgT", [88, 512], BF16))
            hgL = ec7(nc.sbuf_tensor("hgL", [88, 512], BF16))
            hgF = ec7(nc.sbuf_tensor("hgF", [88, 512], F32))
            yf_sb = ec7(nc.sbuf_tensor("yf_sb", [128, 32 * 512], BF16))
            ytok = ec7(nc.sbuf_tensor("ytok", [128, DM], BF16))
            x1r = ec7(nc.sbuf_tensor("x1r", [128, DM], F32))
            yout = ec7(nc.sbuf_tensor("yout", [128, DM], F32))

            ps6 = [ec7(nc.psum_tensor(f"ps6_{i}", [128, 512], F32))
                   for i in range(4)]
            psh = ec7(nc.psum_tensor("psh", [88, 512], F32))
            psb6 = [ec7(nc.psum_tensor(f"psb6_{i}", [128, 512], BF16))
                    for i in range(2)]

            s_p6h = sm["s_p6h"]; s_p6sd = sm["s_p6sd"]; s_p6x = sm["s_p6x"]
            wsems = [sm["s_p6w0"], sm["s_p6w1"], sm["s_p6w2"]]
            @blk.sync
            def _(sy):
                sy.wait_ge(s_hst.h, s_hst.v)
                sy.dma_start(out=qzd_sb[0:NGD, :], in_=ap["qzd"][:, :]
                             ).then_inc(s_p6sd.h, 16)
                s_p6sd.v += 16

            for th in range(2):
                MM_PREV_TH = s_p6p.v
                @blk.sync
                def _(sy, th=th, MM_PREV_TH=MM_PREV_TH):
                    if th == 1:
                        sy.wait_ge(s_p6p.h, MM_PREV_TH)
                    for c in range(NFPC):
                        sy.dma_start(out=hh[:, 512 * c:512 * (c + 1)],
                                     in_=hdram[c, :, 512 * th:512 * (th + 1)]
                                     ).then_inc(s_p6h.h, 16)
                        s_p6h.v += 16
                H_LD = s_p6h.v

                @blk.tensor
                def _(t, H_LD=H_LD):
                    t.wait_ge(s_p6h.h, H_LD)
                    for c in range(11):
                        for n in range(8):
                            fpc = n * 11 + c
                            mmi = t.matmul(psh[:, :], donesb[:, 88 * c:88 * (c + 1)],
                                     hh[:, 512 * fpc:512 * (fpc + 1)],
                                     start=(c == 0 and n == 0),
                                     stop=(c == 10 and n == 7))
                    mmi.then_inc(s_p6p.h, 1)
                    s_p6p.v += 1
                HG = s_p6p.v

                @blk.vector
                def _(v, HG=HG):
                    v.wait_ge(s_p6p.h, HG)
                    v.tensor_scalar(hgF[:, :], psh[:, :], -1.0, None, OP.mult)
                    v.drain()
                    v.tensor_copy(hgT[:, :], hgF[:, :])
                    v.drain()
                    v.tensor_tensor(hgF[:, :], hgF[:, :], hgT[:, :], OP.subtract)
                    v.drain()
                    v.tensor_copy(hgL[:, :], hgF[:, :])
                    v.drain().then_inc(s_p6v.h, 1)
                    s_p6v.v += 1

                for ht in range(8):
                    MM_PREV = s_p6p.v
                    @blk.sync
                    def _(sy, ht=ht, MM_PREV=MM_PREV):
                        sy.wait_ge(s_p6v.h, s_p6v.v)
                        sy.dma_start(out=sd_sb[0:NGD, :],
                                     in_=ap["sdown"][:, 512 * ht:512 * (ht + 1)]
                                     ).then_inc(s_p6sd.h, 16)
                        s_p6sd.v += 16
                    SD_LD = s_p6sd.v

                    @blk.vector
                    def _(v, ht=ht, SD_LD=SD_LD, MM_PREV=MM_PREV):
                        v.wait_ge(s_p6sd.h, SD_LD)
                        v.wait_ge(s_p6p.h, MM_PREV)
                        v.memset(bd_sb[:, :], 0.0)
                        v.drain()
                        for n in range(8):
                            v.tensor_scalar(
                                zd_sb[0:NGD, n:512:8],
                                qzd_sb[0:NGD, 64 * ht:64 * (ht + 1)],
                                28 - 4 * n, 28,
                                OP.logical_shift_left, OP.logical_shift_right)
                        v.drain()
                        v.scalar_tensor_tensor(bd_sb[0:NGD, :], zd_sb[0:NGD, :],
                                               1.0, sd_sb[0:NGD, :],
                                               OP.add, OP.mult)
                        v.drain().then_inc(s_p6v.h, 1)
                        s_p6v.v += 1
                    BD = s_p6v.v
                    if dbg and (dbg["mask"] & 8) and th == 0 and ht == 0:
                        @blk.sync
                        def _(sy, BD=BD):
                            sy.wait_ge(s_p6v.h, BD)
                            sy.dma_start(out=dbg["d_bd"][:, :], in_=bd_sb[:, :]
                                         ).then_inc(s_out.h, 16)
                            s_out.v += 16
                            sy.dma_start(out=dbg["d_zd"][:, :], in_=zd_sb[:, :]
                                         ).then_inc(s_out.h, 16)
                            s_out.v += 16
                            sy.dma_start(out=dbg["d_hg"][:, :], in_=hgT[:, :]
                                         ).then_inc(s_out.h, 16)
                            s_out.v += 16

                    for c in range(NFPC):
                        slot = wdt6[c % 3]
                        wsem = wsems[c % 3]
                        @blk.sync
                        def _(sy, c=c, ht=ht, th=th, slot=slot,
                              MM_PREV=MM_PREV):
                            if c >= 3:
                                sy.wait_ge(s_p6p.h, s_p6p.v - 2)
                            elif ht > 0 or th > 0:
                                sy.wait_ge(s_p6p.h, MM_PREV)
                            sy.dma_start(out=slot[:, :],
                                         in_=wdd[c, :, 512 * ht:512 * (ht + 1)]
                                         ).then_inc(wsem.h, 16)
                            wsem.v += 16
                        WD = wsem.v

                        EV_PREV = s_p6s.v
                        @blk.tensor
                        def _(t, c=c, slot=slot, WD=WD, BD=BD, wsem=wsem,
                              EV_PREV=EV_PREV):
                            t.wait_ge(wsem.h, WD)
                            if c == 0:
                                t.wait_ge(s_p6v.h, BD)
                                if EV_PREV > 0:
                                    t.wait_ge(s_p6s.h, EV_PREV)
                            for m in range(4):
                                mmi = t.matmul(ps6[m][:, :],
                                         slot[:, 128 * m:128 * (m + 1)],
                                         hh[:, 512 * c:512 * (c + 1)],
                                         start=(c == 0), stop=False)
                            if c == NFPC - 1 and os.environ.get("KCOR", "1") == "1":
                                for m in range(4):
                                    mmi = t.matmul(ps6[m][:, :],
                                             bd_sb[:, 128 * m:128 * (m + 1)],
                                             hgT[:, :],
                                             start=False, stop=False)
                                    mmi = t.matmul(ps6[m][:, :],
                                             bd_sb[:, 128 * m:128 * (m + 1)],
                                             hgL[:, :],
                                             start=False, stop=True)
                            elif c == NFPC - 1:
                                for m in range(4):
                                    mmi = t.matmul(ps6[m][:, :],
                                             slot[:, 128 * m:128 * (m + 1)],
                                             hh[:, 512 * c:512 * (c + 1)],
                                             start=False, stop=True)
                            mmi.then_inc(s_p6p.h, 1)
                            s_p6p.v += 1
                    DONE = s_p6p.v

                    @blk.scalar
                    def _(sc, ht=ht, DONE=DONE):
                        sc.wait_ge(s_p6p.h, DONE)
                        for m in range(4):
                            mm = 4 * ht + m
                            sc.copy(yf_sb[:, 512 * mm:512 * (mm + 1)],
                                    ps6[m][:, :])
                        sc.drain().then_inc(s_p6s.h, 1)
                        s_p6s.v += 1

                YF = s_p6s.v
                if dbg and (dbg["mask"] & 16):
                    @blk.sync
                    def _(sy, th=th, YF=YF):
                        sy.wait_ge(s_p6s.h, YF)
                        sy.dma_start(out=dbg["d_yf"][th, :, :],
                                     in_=yf_sb[:, 0:512]).then_inc(s_out.h, 16)
                        s_out.v += 16
                for tc in range(4):
                    trow = 512 * th + 128 * tc
                    @blk.sync
                    def _(sy, trow=trow):
                        if s_out.v > 0:
                            sy.wait_ge(s_out.h, s_out.v)
                        sy.dma_start(out=x1r[:, :],
                                     in_=x1d[trow:trow + 128, :]
                                     ).then_inc(s_p6x.h, 16)
                        s_p6x.v += 16
                    XL = s_p6x.v

                    ytok_done = s_p6v.v
                    cp_marks = [None, None]
                    for mm in range(32):
                        pb6 = psb6[mm % 2]
                        @blk.tensor
                        def _(t, tc=tc, mm=mm, YF=YF, pb6=pb6,
                              cp=cp_marks[mm % 2], yd=ytok_done):
                            t.wait_ge(s_p6s.h, YF)
                            if mm == 0:
                                t.wait_ge(s_p6v.h, yd)
                            if cp is not None:
                                t.wait_ge(s_p6v.h, cp)
                            t.transpose(
                                pb6[:, 0:128],
                                yf_sb[:, 512 * mm + 128 * tc:512 * mm + 128 * tc + 128],
                                ident[:, :]).then_inc(s_p6p.h, 1)
                            s_p6p.v += 1

                        @blk.vector
                        def _(v, mm=mm, pb6=pb6, TRM=s_p6p.v):
                            v.wait_ge(s_p6p.h, TRM)
                            v.tensor_copy(ytok[:, 128 * mm:128 * (mm + 1)],
                                          pb6[:, 0:128]
                                          ).then_inc(s_p6v.h, 1)
                            s_p6v.v += 1
                        cp_marks[mm % 2] = s_p6v.v

                    @blk.vector
                    def _(v, XL=XL, CP=s_p6v.v):
                        v.wait_ge(s_p6v.h, CP)
                        v.wait_ge(s_p6x.h, XL)
                        v.tensor_tensor(x1r[:, :], x1r[:, :], ytok[:, :], OP.add)
                        v.drain()
                        for j in range(8):
                            v.bn_stats(stats[:, 6 * j:6 * (j + 1)],
                                       x1r[:, 512 * j:512 * (j + 1)])
                        v.drain()
                        v.bn_aggr(mv[:, :], stats[:, 0:48])
                        v.drain()
                        v.tensor_scalar(veps[:, :], mv[:, 1:2], EPS, None, OP.add)
                        v.drain().then_inc(s_p6v.h, 1)
                        s_p6v.v += 1

                    @blk.scalar
                    def _(sc):
                        sc.wait_ge(s_p6v.h, s_p6v.v)
                        sc.activation(stdv[:, :], veps[:, :], AF.Sqrt
                                      ).then_inc(s_p6sq.h, 1)
                        s_p6sq.v += 1

                    @blk.vector
                    def _(v, SQ=s_p6sq.v):
                        v.wait_ge(s_p6sq.h, SQ)
                        v.reciprocal(rstd[:, :], stdv[:, :])
                        v.drain()
                        v.tensor_scalar(yout[:, :], x1r[:, :], mv[:, 0:1],
                                        rstd[:, :], OP.subtract, OP.mult)
                        v.drain().then_inc(s_p6v.h, 1)
                        s_p6v.v += 1

                    @blk.sync
                    def _(sy, trow=trow):
                        sy.wait_ge(s_p6v.h, s_p6v.v)
                        sy.dma_start(out=y[trow:trow + 128, :],
                                     in_=yout[:, :]).then_inc(s_out.h, 16)
                        s_out.v += 16

            @blk.sync
            def _(sy):
                sy.wait_ge(s_out.h, s_out.v)


def _dummy_out(nc, blk, y, s_out, wait=()):
    with nc.sbuf_tensor("dmy_out", [128, DM], F32) as dummy:
        @blk.vector
        def _(v):
            v.memset(dummy[:, :], 0.0)
            v.drain().then_inc(s_out.h, 1)
            s_out.v += 1

        @blk.sync
        def _(sy):
            for h, val in wait:
                sy.wait_ge(h, val)
            sy.wait_ge(s_out.h, s_out.v)
            for tt in range(8):
                sy.dma_start(out=y[128 * tt:128 * (tt + 1), :],
                             in_=dummy[:, :]).then_inc(s_out.h, 16)
                s_out.v += 16
            sy.wait_ge(s_out.h, s_out.v)


def _qkv(nc, blk, ap, b, wkv, xst, ropes, q_sb, k_sb, vtok, qraw, rt1, rt2,
         ps, psb, ident, prot, sm, CONST):
    s_wkv = sm["s_wkv"]; s_x = sm["s_x"]; s_rope = sm["s_rope"]
    s_qkps = sm["s_qkps"]; s_ev = sm["s_ev"]; s_rp = sm["s_rp"]
    s_const = sm["s_const"]; s_wo = sm["s_wo"]; s_sc = sm["s_sc"]
    s_ve2 = sm["s_ve2"]
    x_sems = [s_x, sm["s_x2"], sm["s_x3"], sm["s_x4"]]

    @blk.gpsimd
    def _(g, b=b):
        if b == 1:
            g.wait_ge(s_qkps.h, s_qkps.v)
            g.wait_ge(s_wo.h, s_wo.v)
        for hc in range(32):
            g.dma_start(out=wkv[:, 768 * hc:768 * (hc + 1)],
                        in_=ap["wkvT"][128 * hc:128 * (hc + 1), :]
                        ).then_inc(s_wkv.h, 16)
            s_wkv.v += 16
    WKV = s_wkv.v

    rp_slot = [0] * 6

    for st in range(8):
        t0 = b * S + 512 * st
        p0 = 512 * st
        @blk.gpsimd
        def _(g, t0=t0, st=st, b=b):
            if st > 0 or b == 1:
                g.wait_ge(s_qkps.h, s_qkps.v)
            for grp in range(4):
                for hc in range(8 * grp, 8 * grp + 8):
                    g.dma_start(out=xst[:, 512 * hc:512 * (hc + 1)],
                                in_=ap["xT"][128 * hc:128 * (hc + 1), t0:t0 + 512]
                                ).then_inc(x_sems[grp].h, 16)
                    x_sems[grp].v += 16
        XG = [x_sems[g_].v for g_ in range(4)]

        @blk.sync
        def _(sy, p0=p0, st=st, b=b):
            if st > 0 or b == 1:
                sy.wait_ge(s_rp.h, s_rp.v)
            sy.dma_start(out=ropes[:, 0:512],
                         in_=ap["cos_t"][:, p0:p0 + 512]).then_inc(s_rope.h, 16)
            s_rope.v += 16
            sy.dma_start(out=ropes[:, 512:1024],
                         in_=ap["sin_t"][:, p0:p0 + 512]).then_inc(s_rope.h, 16)
            s_rope.v += 16
        RP = s_rope.v

        if KQKV < 2:
            continue
        @blk.tensor
        def _(t, st=st, b=b, XG=XG):
            t.wait_ge(s_wkv.h, WKV)
            if st > 0 or b == 1:
                t.wait_ge(s_ev.h, s_ev.v)
                t.wait_ge(s_rp.h, s_rp.v)
            if b == 1 and st == 0:
                t.wait_ge(s_sc.h, s_sc.v)
                t.wait_ge(s_ve2.h, s_ve2.v)
                t.wait_ge(sm["s_woe"].h, sm["s_woe"].v)
            for oc in range(6):
                for hc in range(32):
                    if oc == 0 and hc % 8 == 0:
                        t.wait_ge(x_sems[hc // 8].h, XG[hc // 8])
                    mmi = t.matmul(ps[oc][:, :],
                             wkv[:, 768 * hc + 128 * oc:768 * hc + 128 * (oc + 1)],
                             xst[:, 512 * hc:512 * (hc + 1)],
                             start=(hc == 0), stop=(hc == 31))
                mmi.then_inc(s_qkps.h, 1)
                s_qkps.v += 1
        QKB = s_qkps.v - 6

        if KQKV < 3:
            continue
        for oc in range(6):
            if os.environ.get("KEVAC", "s") == "v":
                @blk.vector
                def _(v, oc=oc, QKB=QKB, rp_need=rp_slot[oc]):
                    v.wait_ge(s_qkps.h, QKB + oc + 1)
                    if rp_need > 0:
                        v.wait_ge(s_rp.h, rp_need)
                    v.tensor_copy(qraw[:, 512 * oc:512 * (oc + 1)], ps[oc][:, :])
                    v.drain().then_inc(s_ev.h, 1)
                    s_ev.v += 1
            else:
                @blk.scalar
                def _(sc, oc=oc, QKB=QKB, rp_need=rp_slot[oc]):
                    sc.wait_ge(s_qkps.h, QKB + oc + 1)
                    if rp_need > 0:
                        sc.wait_ge(s_rp.h, rp_need)
                    sc.copy(qraw[:, 512 * oc:512 * (oc + 1)], ps[oc][:, :])
                    sc.drain().then_inc(s_ev.h, 1)
                    s_ev.v += 1
            EV = s_ev.v

            if KQKV < 4 or (oc == 5 and KQKV < 5):
                continue
            if oc < 5:
                @blk.tensor
                def _(t, oc=oc, EV=EV):
                    t.wait_ge(s_ev.h, EV)
                    t.wait_ge(s_const.h, CONST)
                    t.matmul(ps[oc][:, :], prot[:, :],
                             qraw[:, 512 * oc:512 * (oc + 1)],
                             start=True, stop=True).then_inc(s_qkps.h, 1)
                    s_qkps.v += 1
                PR = s_qkps.v

                dst = (q_sb[:, S * oc + 512 * st:S * oc + 512 * (st + 1)]
                       if oc < 4 else k_sb[:, 512 * st:512 * (st + 1)])
                @blk.vector
                def _(v, oc=oc, dst=dst, PR=PR, RP=RP):
                    v.wait_ge(s_qkps.h, PR)
                    v.wait_ge(s_rope.h, RP)
                    v.tensor_tensor(rt1[:, :], qraw[:, 512 * oc:512 * (oc + 1)],
                                    ropes[:, 0:512], OP.mult)
                    v.tensor_tensor(rt2[:, :], ps[oc][:, :],
                                    ropes[:, 512:1024], OP.mult)
                    v.drain()
                    v.tensor_tensor(dst, rt1[:, :], rt2[:, :], OP.add)
                    v.drain().then_inc(s_rp.h, 1)
                    s_rp.v += 1
                rp_slot[oc] = s_rp.v
            else:
                @blk.tensor
                def _(t, st=st, EV=EV):
                    t.wait_ge(s_ev.h, EV)
                    t.wait_ge(s_const.h, CONST)
                    if s_rp.v > 0:
                        t.wait_ge(s_rp.h, s_rp.v)
                    for kc in range(4):
                        tri = t.transpose(psb[:, 128 * kc:128 * (kc + 1)],
                                    qraw[:, 512 * 5 + 128 * kc:512 * 5 + 128 * (kc + 1)],
                                    ident[:, :])
                    tri.then_inc(s_qkps.h, 1)
                    s_qkps.v += 1
                VT = s_qkps.v

                @blk.vector
                def _(v, st=st, VT=VT):
                    v.wait_ge(s_qkps.h, VT)
                    v.tensor_copy(vtok[:, 512 * st:512 * (st + 1)], psb[:, :])
                    v.drain().then_inc(s_rp.h, 1)
                    s_rp.v += 1
                rp_slot[5] = s_rp.v


def _attn(nc, blk, b, q_sb, k_sb, vtok, outT, scrs, prb, prT, maskb,
          rmax, nmax, ssum, rsum, ps, psbp, ident, sm):
    s_pe2 = sm["s_pe2"]; s_sc = sm["s_sc"]; s_ve2 = sm["s_ve2"]
    s_rp = sm["s_rp"]; s_qkps = sm["s_qkps"]

    s_msk = sm["s_msk"]
    MASK_MARK = getattr(s_msk, "mask_mark", 0)
    qk_banks = [0, 1, 3, 4]
    bank_evac = [None, None, None, None]
    bank_i = 0
    first_diag = [True]
    RP_ALL = s_rp.v

    exp_hist = getattr(s_sc, "attn_exp_hist", [])
    qti = [getattr(s_sc, "attn_qti", 0)]
    pend_softmax = [None]
    for h in range(4):
        for qg in range(8):
            ktiles = qg + 1
            pe_group0 = s_pe2.v
            sc_group0 = s_sc.v
            kw = 512 * ktiles
            for qt in range(4):
                q0 = 512 * qg + 128 * qt
                scr = scrs[qti[0] % 2]
                qti[0] += 1
                for kt in range(ktiles):
                    bi = bank_i % 4
                    bank_i += 1
                    bank = ps[qk_banks[bi]]
                    need = bank_evac[bi]
                    @blk.tensor
                    def _(t, h=h, q0=q0, kt=kt, bank=bank, need=need,
                          first=(h == 0 and qg == 0 and qt == 0 and kt == 0)):
                        if first:
                            t.wait_ge(s_rp.h, RP_ALL)
                        if need is not None:
                            t.wait_ge(s_ve2.h if need[0] else s_sc.h, need[1])
                        t.matmul(bank[:, :],
                                 q_sb[:, S * h + q0:S * h + q0 + 128],
                                 k_sb[:, 512 * kt:512 * (kt + 1)],
                                 start=True, stop=True).then_inc(s_pe2.h, 1)
                        s_pe2.v += 1
                    PE = s_pe2.v

                    diag = (kt == ktiles - 1)
                    nq = qti[0] - 1
                    prev_exp = exp_hist[nq - 2] if nq >= 2 else 0
                    if diag:
                        @blk.vector
                        def _(v, qt=qt, kt=kt, bank=bank, PE=PE, scr=scr,
                              prev_exp=prev_exp, fd=first_diag[0]):
                            if fd:
                                v.wait_ge(s_msk.h, MASK_MARK)
                            v.wait_ge(s_pe2.h, PE)
                            if prev_exp > 0:
                                v.wait_ge(s_sc.h, prev_exp)
                            v.tensor_tensor(scr[:, 512 * kt:512 * (kt + 1)],
                                            bank[:, :],
                                            maskb[:, 512 * qt:512 * (qt + 1)],
                                            OP.add)
                            v.drain().then_inc(s_ve2.h, 1)
                            s_ve2.v += 1
                        bank_evac[bi] = (True, s_ve2.v)
                        first_diag[0] = False
                    else:
                        @blk.scalar
                        def _(sc, kt=kt, bank=bank, PE=PE, scr=scr,
                              prev_exp=prev_exp):
                            sc.wait_ge(s_pe2.h, PE)
                            if prev_exp > 0:
                                sc.wait_ge(s_sc.h, prev_exp)
                            sc.copy(scr[:, 512 * kt:512 * (kt + 1)], bank[:, :])
                            sc.drain().then_inc(s_sc.h, 1)
                            s_sc.v += 1
                        bank_evac[bi] = (False, s_sc.v)

                def softmax_emit(qt=qt, kw=kw, EVAC_S=s_sc.v, EVAC_V=s_ve2.v,
                                 pe_group0=pe_group0, scr=scr):
                    @blk.vector
                    def _(v):
                        v.wait_ge(s_sc.h, EVAC_S)
                        v.wait_ge(s_ve2.h, EVAC_V)
                        v.reduce_max(rmax[:, qt:qt + 1], scr[:, 0:kw], axis=AXX)
                        v.drain()
                        v.tensor_scalar(nmax[:, qt:qt + 1], rmax[:, qt:qt + 1],
                                        -INV_SQRT_HD, None, OP.mult)
                        v.drain().then_inc(s_ve2.h, 1)
                        s_ve2.v += 1
                    NM = s_ve2.v

                    @blk.scalar
                    def _(sc):
                        sc.wait_ge(s_ve2.h, NM)
                        sc.wait_ge(s_pe2.h, pe_group0)
                        sc.activation(prb[:, S * qt:S * qt + kw],
                                      scr[:, 0:kw],
                                      AF.Exp, bias=nmax[:, qt:qt + 1],
                                      scale=np.float32(INV_SQRT_HD),
                                      accum_out=ssum[:, qt:qt + 1]
                                      ).then_inc(s_sc.h, 1)
                        s_sc.v += 1
                    exp_hist.append(s_sc.v)
                    EX = s_sc.v

                    @blk.vector
                    def _(v):
                        v.wait_ge(s_sc.h, EX)
                        v.reciprocal(rsum[:, qt:qt + 1], ssum[:, qt:qt + 1])
                        v.drain()
                        v.tensor_scalar(prb[:, S * qt:S * qt + kw],
                                        prb[:, S * qt:S * qt + kw],
                                        rsum[:, qt:qt + 1], None, OP.mult)
                        v.drain().then_inc(s_ve2.h, 1)
                        s_ve2.v += 1
                if pend_softmax[0] is not None:
                    pend_softmax[0]()
                pend_softmax[0] = softmax_emit
            pend_softmax[0]()
            pend_softmax[0] = None
            PROBS = s_ve2.v

            if int(os.environ.get("KATT", "9")) < 3:
                continue
            av_bank = ps[2]
            nkc = 4 * ktiles
            for kc in range(nkc):
                pb = psbp[kc % 2]
                half = (kc % 2) * 512
                @blk.tensor
                def _(t, kc=kc, pb=pb, PROBS=PROBS, sc_group0=sc_group0):
                    if kc == 0:
                        t.wait_ge(s_ve2.h, PROBS)
                        if sc_group0 > 0:
                            t.wait_ge(s_sc.h, sc_group0)
                    if kc >= 2:
                        t.wait_ge(s_ve2.h, PROBS + kc - 1)
                    for qt in range(4):
                        tri = t.transpose(pb[:, 128 * qt:128 * (qt + 1)],
                                    prb[:, S * qt + 128 * kc:S * qt + 128 * (kc + 1)],
                                    ident[:, :])
                    tri.then_inc(s_pe2.h, 1)
                    s_pe2.v += 1
                TR = s_pe2.v

                @blk.vector
                def _(v, half=half, pb=pb, TR=TR):
                    v.wait_ge(s_pe2.h, TR)
                    v.tensor_copy(prT[:, half:half + 512],
                                  pb[:, :]).then_inc(s_ve2.h, 1)
                    s_ve2.v += 1
                TC = s_ve2.v

                if int(os.environ.get("KATT", "9")) < 4:
                    continue
                @blk.tensor
                def _(t, kc=kc, half=half, nkc=nkc, TC=TC, av_bank=av_bank):
                    t.wait_ge(s_ve2.h, TC)
                    mmi = t.matmul(av_bank[:, :],
                             vtok[:, 128 * kc:128 * (kc + 1)],
                             prT[:, half:half + 512],
                             start=(kc == 0), stop=(kc == nkc - 1))
                    if kc == nkc - 1:
                        mmi.then_inc(s_pe2.h, 1)
                        s_pe2.v += 1
            AV = s_pe2.v

            @blk.scalar
            def _(sc, h=h, qg=qg, AV=AV, av_bank=av_bank):
                sc.wait_ge(s_pe2.h, AV)
                sc.copy(outT[:, S * h + 512 * qg:S * h + 512 * (qg + 1)],
                        av_bank[:, :])
                sc.drain().then_inc(s_sc.h, 1)
                s_sc.v += 1
    s_sc.attn_exp_hist = exp_hist
    s_sc.attn_qti = qti[0]


def _wo(nc, blk, ap, b, outT, woB, wosb, cc_in, ps, sm):
    s_wkv = sm["s_wkv"]; s_wo = sm["s_wo"]; s_st = sm["s_st"]
    s_sc = sm["s_sc"]; s_woe = sm["s_woe"]; s_st2 = sm["s_st2"]
    st_sems = [s_st, s_st2]

    SC_END = s_sc.v

    @blk.gpsimd
    def _(g, b=b, WOE0=s_woe.v):
        g.wait_ge(s_sc.h, SC_END)
        if b == 1:
            g.wait_ge(s_wo.h, s_wo.v)
            g.wait_ge(s_woe.h, WOE0)
        for oc in range(4):
            g.dma_start(out=woB[:, DM * oc:DM * (oc + 1)],
                        in_=ap["woT"][128 * oc:128 * (oc + 1), :]
                        ).then_inc(s_wkv.h, 16)
            s_wkv.v += 16
    WOL = s_wkv.v

    evac_mark = getattr(s_woe, "wo_evac_marks", [0, 0])
    store_mark = getattr(s_st, "wo_store_marks", [0, 0])

    idx = 0
    for m in range(32):
        for tt in range(8):
            sl = idx % 2
            idx += 1
            bank = ps[sl]
            need_ev = evac_mark[sl]
            @blk.tensor
            def _(t, m=m, tt=tt, bank=bank, need_ev=need_ev,
                  first=(m == 0 and tt == 0)):
                t.wait_ge(s_wkv.h, WOL)
                if first:
                    t.wait_ge(s_sc.h, SC_END)
                if need_ev > 0:
                    t.wait_ge(s_woe.h, need_ev)
                for oc in range(4):
                    mmi = t.matmul(bank[:, :],
                             woB[:, DM * oc + 128 * m:DM * oc + 128 * (m + 1)],
                             outT[:, S * oc + 512 * tt:S * oc + 512 * (tt + 1)],
                             start=(oc == 0), stop=(oc == 3))
                mmi.then_inc(s_wo.h, 1)
                s_wo.v += 1
            WOP = s_wo.v

            need_st = store_mark[sl]
            @blk.scalar
            def _(sc, bank=bank, sl=sl, WOP=WOP, need_st=need_st):
                sc.wait_ge(s_wo.h, WOP)
                if need_st > 0:
                    sc.wait_ge(st_sems[sl].h, need_st)
                sc.copy(wosb[:, 512 * sl:512 * (sl + 1)], bank[:, :])
                sc.drain().then_inc(s_woe.h, 1)
                s_woe.v += 1
            evac_mark[sl] = s_woe.v
            WOE = s_woe.v

            @blk.sync
            def _(sy, m=m, tt=tt, b=b, sl=sl, WOE=WOE):
                sy.wait_ge(s_woe.h, WOE)
                gt = b * S + 512 * tt
                r = gt // TPC
                tl = gt % TPC
                sy.dma_start(out=cc_in[DM * r + 128 * m:DM * r + 128 * (m + 1),
                                       tl:tl + 512],
                             in_=wosb[:, 512 * sl:512 * (sl + 1)]
                             ).then_inc(st_sems[sl].h, 16)
                st_sems[sl].v += 16
            store_mark[sl] = st_sems[sl].v
    s_woe.wo_evac_marks = evac_mark
    s_st.wo_store_marks = store_mark


def kernel(**inputs):
    per_core = _host_prep(inputs)
    nc = build_program()
    res = run_bass_kernel_spmd(nc, per_core, core_ids=list(range(8)))
    outs = [res.results[c]["y"] for c in range(8)]
    full = np.concatenate(outs, axis=0)
    return full.reshape(B, S, DM)
